# version 03
# baseline (speedup 1.0000x reference)
import numpy as np
import concourse.bass as bass
import concourse.mybir as mybir
from concourse.bass_utils import run_bass_kernel_spmd

F32 = mybir.dt.float32
BF16 = mybir.dt.bfloat16
U8 = mybir.dt.uint8
AF = mybir.ActivationFunctionType
ALU = mybir.AluOpType
AX = mybir.AxisListType

N_CORES = 8
T_CORE = 2048
N_PASS = 2
T_PASS = 1024
NT = T_PASS // 128
NB = T_PASS // 512
HC = 16
ICE = 11
NEXP = 16
EPS = 1e-5
DCH = 1
NDC = 64 // DCH

ENG = ("sync", "vector", "scalar", "tensor", "gpsimd")
ESEM = {"vector": "v", "scalar": "s", "tensor": "t", "gpsimd": "g"}


class Buf:
    __slots__ = ("name", "w", "readers")

    def __init__(self, name):
        self.name = name
        self.w = None
        self.readers = []


class Plan:
    def __init__(self):
        self.ops = {e: [] for e in ENG}
        self.cnt = {}
        self.known = {e: {} for e in ENG}

    def emit(self, eng, fn, reads=(), writes=(), dma=None, extra=()):
        waits = {}

        def need(sw):
            if sw is None:
                return
            s, v = sw
            if self.known[eng].get(s, 0) >= v:
                return
            if waits.get(s, 0) < v:
                waits[s] = v

        for b in reads:
            need(b.w)
        for b in writes:
            need(b.w)
            for sw in b.readers:
                need(sw)
        for sw in extra:
            need(sw)
        sem = dma if dma else ESEM[eng]
        amt = 16 if dma else 1
        self.cnt[sem] = self.cnt.get(sem, 0) + amt
        v = self.cnt[sem]
        for s, w in waits.items():
            self.known[eng][s] = max(self.known[eng].get(s, 0), w)
        self.ops[eng].append((fn, sorted(waits.items()), sem, amt))
        for b in writes:
            b.w = (sem, v)
            b.readers = []
        for b in reads:
            b.readers.append((sem, v))
        return (sem, v)

    def wait_only(self, eng, pairs):
        waits = {}
        for s, v in pairs:
            if self.known[eng].get(s, 0) >= v:
                continue
            waits[s] = max(waits.get(s, 0), v)
        if not waits:
            return
        for s, w in waits.items():
            self.known[eng][s] = max(self.known[eng].get(s, 0), w)
        self.ops[eng].append((None, sorted(waits.items()), None, 0))


class Arena:
    SIZES = {F32: 4, BF16: 2, U8: 1}

    def __init__(self, ap, nbytes):
        self.base = ap
        self.nbytes = nbytes
        self.off = 0
        self.ghost = {}
        self.live = []
        self.peak = 0

    def reset(self):
        deps = dict(self.ghost)
        for b in self.live:
            sws = ([b.w] if b.w else []) + b.readers
            for s, v in sws:
                deps[s] = max(deps.get(s, 0), v)
        self.ghost = deps
        self.live = []
        self.peak = max(self.peak, self.off)
        self.off = 0

    def alloc(self, name, parts, n, dtype):
        nbytes = n * self.SIZES[dtype]
        off = (self.off + 3) & ~3
        assert off + nbytes <= self.nbytes, (
            f"arena overflow at {name}: {off + nbytes} > {self.nbytes}")
        self.off = off + nbytes
        view = self.base[0:parts, off:off + nbytes].bitcast(dtype)
        b = Buf(name)
        if self.ghost:
            b.readers = list(self.ghost.items())
        self.live.append(b)
        return view, b


def build_program(debug=False):
    nc = bass.Bass("TRN2", target_bir_lowering=False)

    xin = nc.dram_tensor("xin", [T_CORE, 2048], F32, kind="ExternalInput").ap()
    wg_d = nc.dram_tensor("w_gate", [16, 2048, 1408], F32, kind="ExternalInput").ap()
    wu_d = nc.dram_tensor("w_up", [16, 2048, 1408], F32, kind="ExternalInput").ap()
    wd_d = nc.dram_tensor("w_down", [16, 1408, 2048], F32, kind="ExternalInput").ap()
    swg_d = nc.dram_tensor("sw_gate", [2048, 2816], F32, kind="ExternalInput").ap()
    swu_d = nc.dram_tensor("sw_up", [2048, 2816], F32, kind="ExternalInput").ap()
    swd_d = nc.dram_tensor("sw_down", [2816, 2048], F32, kind="ExternalInput").ap()
    wqkvbd_d = nc.dram_tensor("wqkv_bd", [16, 128, 384], F32, kind="ExternalInput").ap()
    biasp_d = nc.dram_tensor("bias_pairs", [1, 6144], F32, kind="ExternalInput").ap()
    woutbd_d = nc.dram_tensor("wout_bd", [128, 128], F32, kind="ExternalInput").ap()
    bout2_d = nc.dram_tensor("bout2", [128, 1], F32, kind="ExternalInput").ap()
    gwt_d = nc.dram_tensor("gate_wT", [2048, 16], F32, kind="ExternalInput").ap()
    eye_d = nc.dram_tensor("eye128", [128, 128], F32, kind="ExternalInput").ap()
    out_d = nc.dram_tensor("out", [T_CORE, 2048], F32, kind="ExternalOutput").ap()
    wgsc_d = nc.dram_tensor("wgsc", [18, 11, 128, 2048], BF16, kind="Internal").ap()
    wusc_d = nc.dram_tensor("wusc", [18, 11, 128, 2048], BF16, kind="Internal").ap()
    wdsc_d = nc.dram_tensor("wdsc", [18, 16, 128, 1408], BF16, kind="Internal").ap()
    if debug:
        dbgA = [nc.dram_tensor(f"dbgA{h}", [128, 16384], F32,
                               kind="ExternalOutput").ap() for h in range(2)]
        dbgL = [nc.dram_tensor(f"dbgL{h}", [128, 16384], BF16,
                               kind="ExternalOutput").ap() for h in range(2)]
        dbgC = [nc.dram_tensor(f"dbgC{h}", [128, 128], F32,
                               kind="ExternalOutput").ap() for h in range(2)]
        dbgQ = nc.dram_tensor("dbgQ", [128, 6144], BF16,
                              kind="ExternalOutput").ap()
        dbgX = nc.dram_tensor("dbgX", [128, 2048], F32,
                              kind="ExternalOutput").ap()

    plan = Plan()
    from contextlib import ExitStack
    with ExitStack() as st:
        def sbt(name, shape, dtype):
            return st.enter_context(nc.sbuf_tensor(name, shape, dtype))

        A_t = sbt("A", [128, HC * T_PASS], F32)
        wqkvbd_t = sbt("wqkvbd", [128, 6144], BF16)
        woutbd_t = sbt("woutbd", [128, 128], BF16)
        bout2_t = sbt("bout2_sb", [128, 1], F32)
        gtw_t = sbt("gtw", [128, 256], BF16)
        sel_t = sbt("sel", [16, 2048], BF16)
        eyef_t = sbt("eyef", [128, 128], F32)
        eyeb_t = sbt("eyeb", [128, 128], BF16)
        onescol_t = sbt("onescol", [128, 1], F32)
        onesrowf_t = sbt("onesrowf", [1, 128], F32)
        onesrowb_t = sbt("onesrowb", [1, 128], BF16)
        eps_t = sbt("eps_sb", [128, 1], F32)
        ARENA_BYTES = 116256
        arena_t = sbt("arena", [128, ARENA_BYTES], U8)

        psum_t = [st.enter_context(nc.psum_tensor(f"bank{i}", [128, 512], F32))
                  for i in range(8)]
        banks = [p.ap() for p in psum_t]
        bankB = [Buf(f"bank{i}") for i in range(8)]

        sem_names = ["v", "s", "t", "g", "dS", "dA"]
        sems = {n: st.enter_context(nc.semaphore("sem_" + n)) for n in sem_names}
        block = st.enter_context(nc.Block())

        arena = Arena(arena_t.ap(), ARENA_BYTES)

        A = A_t.ap()
        Ab = [Buf(f"A{k}") for k in range(HC)]
        wgscB = [Buf(f"wgsc{e}") for e in range(NEXP + 2)]
        wuscB = [Buf(f"wusc{e}") for e in range(NEXP + 2)]
        wdscB = [Buf(f"wdsc{e}") for e in range(NEXP + 2)]
        wqkvbd = wqkvbd_t.ap(); wqkvbdB = Buf("wqkvbd")
        woutbd = woutbd_t.ap(); woutbdB = Buf("woutbd")
        bout2 = bout2_t.ap(); bout2B = Buf("bout2")
        gtw = gtw_t.ap(); gtwB = Buf("gtw")
        sel = sel_t.ap(); selB = Buf("sel")
        eyef = eyef_t.ap(); eyefB = Buf("eyef")
        eyeb = eyeb_t.ap(); eyebB = Buf("eyeb")
        onescol = onescol_t.ap(); onescolB = Buf("onescol")
        onesrowf = onesrowf_t.ap(); onesrowfB = Buf("onesrowf")
        onesrowb = onesrowb_t.ap(); onesrowbB = Buf("onesrowb")
        epsv = eps_t.ap(); epsB = Buf("eps")

        E = plan.emit

        stg, stgB = arena.alloc("stg", 128, 6144, F32)
        E("sync", lambda e, d=stg: e.dma_start(out=d, in_=wqkvbd_d.rearrange(
            "k p i -> p k i")), writes=[stgB], dma="dS")
        E("vector", lambda e, s=stg: e.tensor_copy(out=wqkvbd, in_=s),
          reads=[stgB], writes=[wqkvbdB])
        stg3, stg3B = arena.alloc("stg3", 128, 128, F32)
        E("sync", lambda e, d=stg3: e.dma_start(out=d, in_=woutbd_d),
          writes=[stg3B], dma="dS")
        E("vector", lambda e, s=stg3: e.tensor_copy(out=woutbd, in_=s),
          reads=[stg3B], writes=[woutbdB])
        E("sync", lambda e: e.dma_start(out=bout2, in_=bout2_d),
          writes=[bout2B], dma="dS")
        stg4, stg4B = arena.alloc("stg4", 128, 256, F32)
        E("sync", lambda e, d=stg4: e.dma_start(out=d, in_=gwt_d.rearrange(
            "(k p) e -> p k e", p=128)), writes=[stg4B], dma="dS")
        E("vector", lambda e, s=stg4: e.tensor_copy(out=gtw, in_=s),
          reads=[stg4B], writes=[gtwB])
        E("sync", lambda e: e.dma_start(out=eyef, in_=eye_d),
          writes=[eyefB], dma="dS")
        E("vector", lambda e: e.tensor_copy(out=eyeb, in_=eyef),
          reads=[eyefB], writes=[eyebB])
        E("vector", lambda e: e.memset(onescol, 1.0), writes=[onescolB])
        E("vector", lambda e: e.memset(epsv, EPS), writes=[epsB])
        E("vector", lambda e: e.memset(onesrowf, 1.0), writes=[onesrowfB])
        E("vector", lambda e: e.memset(onesrowb, 1.0), writes=[onesrowbB])
        ioe, ioeB = arena.alloc("ioe", 16, 2048, F32)
        iop, iopB = arena.alloc("iop", 16, 2048, F32)
        E("gpsimd", lambda e, d=ioe: e.iota(out=d, pattern=[[1, 16], [0, 128]],
          base=0, channel_multiplier=0, allow_small_or_imprecise_dtypes=True),
          writes=[ioeB])
        E("gpsimd", lambda e, d=iop: e.iota(out=d, pattern=[[0, 16], [0, 128]],
          base=0, channel_multiplier=1, allow_small_or_imprecise_dtypes=True),
          writes=[iopB])
        E("vector", lambda e: e.tensor_tensor(out=sel, in0=ioe, in1=iop,
          op=ALU.is_equal), reads=[ioeB, iopB], writes=[selB])

        qkv3 = None

        for h in range(N_PASS):
            tok0 = h * T_PASS
            arena.reset()
            biasrow, biasrowB = arena.alloc("biasrow", 1, 6144, BF16)
            bstg, bstgB = arena.alloc("bstg", 1, 512, F32)
            for bc in range(12):
                E("sync", lambda e, d=bstg, bc=bc: e.dma_start(
                    out=d, in_=biasp_d[:, bc * 512:(bc + 1) * 512]),
                  writes=[bstgB], dma="dS")
                E("vector", lambda e, s=bstg, bc=bc: e.tensor_copy(
                    out=biasrow[:, bc * 512:(bc + 1) * 512], in_=s),
                  reads=[bstgB], writes=[biasrowB])
            x_tm, x_tmB = zip(*[arena.alloc(f"x_tm{i}", 128, 2048, F32)
                                for i in range(2)])
            qkv_tm, qkv_tmB = arena.alloc("qkv_tm", 128, 6144, BF16)
            xln_tm, xln_tmB = arena.alloc("xln_tm", 128, 2048, BF16)
            xln_fm, xln_fmB = arena.alloc("xln_fm", 128, 2048, BF16)
            S2, S2B = zip(*[arena.alloc(f"S2_{i}", 128, 1024 * DCH, BF16)
                            for i in range(2)])
            P2, P2B = zip(*[arena.alloc(f"P2_{i}", 128, 1024 * DCH, BF16)
                            for i in range(2)])
            tmp2, tmp2B = zip(*[arena.alloc(f"tmp2_{i}", 128, 1024 * DCH, BF16)
                                for i in range(2)])
            sums2, sums2B = zip(*[arena.alloc(f"sums2_{i}", 128, 32 * DCH, F32)
                                  for i in range(2)])
            rc2, rc2B = zip(*[arena.alloc(f"rc2_{i}", 128, 32 * DCH, F32)
                              for i in range(2)])
            v2, v2B = zip(*[arena.alloc(f"v2_{i}", 128, 32 * DCH, BF16)
                            for i in range(2)])
            ctx, ctxB = arena.alloc("ctx", 128, 2048, F32)
            xsq, xsqB = ctx, ctxB
            ctx_fm, ctx_fmB = zip(*[arena.alloc(f"ctxfm{i}", 128, 128, BF16)
                                    for i in range(2)])
            t3, t3B = zip(*[arena.alloc(f"t3_{i}", 128, 128, F32)
                            for i in range(1)]); t3 = t3 * 2; t3B = t3B * 2
            sc = {}
            for nm in ("rs", "rq", "mean", "msq", "m2", "var", "std", "rstd",
                       "mrneg"):
                sc[nm] = arena.alloc("ln1_" + nm, 128, 1, F32)

            prelude_units = []
            if h == 0:
                pf, pfB = zip(*[arena.alloc(f"pf{i}", 128, 2048, F32)
                                for i in range(2)])
                pbv, pbB = zip(*[arena.alloc(f"pb{i}", 128, 2048, BF16)
                                 for i in range(2)])
                pstate = {"n": 0}

                def pre_unit(kind, ex, idx):
                    par = pstate["n"] % 2
                    pstate["n"] += 1
                    ioff = 0 if ex != NEXP + 1 else 11
                    joff = 0 if ex != NEXP + 1 else 1408
                    if kind == "g":
                        if ex < NEXP:
                            srcv = wg_d[ex][idx * 128:(idx + 1) * 128, :]
                        else:
                            srcv = swg_d[idx * 128:(idx + 1) * 128,
                                         ioff * 128:ioff * 128 + 1408]
                        n = 1408
                        eng = "scalar"
                        dst = wgsc_d[ex][:, :, idx * 128:(idx + 1) * 128]
                    elif kind == "u":
                        if ex < NEXP:
                            srcv = wu_d[ex][idx * 128:(idx + 1) * 128, :]
                        else:
                            srcv = swu_d[idx * 128:(idx + 1) * 128,
                                         ioff * 128:ioff * 128 + 1408]
                        n = 1408
                        eng = "gpsimd"
                        dst = wusc_d[ex][:, :, idx * 128:(idx + 1) * 128]
                    else:
                        if ex < NEXP:
                            srcv = wd_d[ex][idx * 128:(idx + 1) * 128, :]
                        else:
                            srcv = swd_d[joff + idx * 128:
                                         joff + (idx + 1) * 128, :]
                        n = 2048
                        eng = "scalar"
                        dst = wdsc_d[ex][:, :, idx * 128:(idx + 1) * 128]
                    fv = pf[par][:, 0:n]
                    bv = pbv[par][:, 0:n]
                    E("sync", lambda e, s=srcv, d=fv: e.dma_start(
                        out=d, in_=s), writes=[pfB[par]], dma="dS")
                    if eng == "scalar":
                        E("scalar", lambda e, s=fv, d=bv: e.activation(
                            out=d, in_=s, func=AF.Identity),
                          reads=[pfB[par]], writes=[pbB[par]])
                    else:
                        E("gpsimd", lambda e, s=fv, d=bv: e.tensor_copy(
                            out=d, in_=s), reads=[pfB[par]], writes=[pbB[par]])
                    if kind == "d":
                        sview = bv.rearrange("p (hh x) -> p hh x", hh=16)
                        dview = dst.rearrange("hh p x -> p hh x")
                        wb = wdscB[ex]
                    else:
                        sview = bv.rearrange("p (i x) -> p i x", i=11)
                        dview = dst.rearrange("i p x -> p i x")
                        wb = wgscB[ex] if kind == "g" else wuscB[ex]
                    E("scalar", lambda e, s=sview, d=dview: e.dma_start(
                        out=d, in_=s), reads=[pbB[par]], writes=[wb], dma="dA")

                for ex in range(NEXP + 2):
                    for k in range(HC):
                        prelude_units.append(("g", ex, k))
                        prelude_units.append(("u", ex, k))
                    for j in range(ICE):
                        prelude_units.append(("d", ex, j))

            for ti in range(NT):
                xb = ti % 2
                xt = x_tm[xb]
                E("sync", lambda e, d=xt, r0=tok0 + ti * 128:
                  e.dma_start(out=d, in_=xin[r0:r0 + 128, :]),
                  writes=[x_tmB[xb]], dma="dS")
                E("vector", lambda e, x=xt: e.reduce_sum(
                    out=sc["rs"][0], in_=x, axis=AX.X),
                  reads=[x_tmB[xb]], writes=[sc["rs"][1]])
                E("vector", lambda e, x=xt, s=xsq: e.scalar_tensor_tensor(
                    out=s, in0=x, scalar=1.0, in1=x, op0=ALU.mult,
                    op1=ALU.mult, accum_out=sc["rq"][0]),
                  reads=[x_tmB[xb]], writes=[xsqB, sc["rq"][1]])
                E("vector", lambda e: e.tensor_scalar_mul(
                    out=sc["mean"][0], in0=sc["rs"][0], scalar1=1.0 / 2048),
                  reads=[sc["rs"][1]], writes=[sc["mean"][1]])
                E("vector", lambda e: e.tensor_scalar_mul(
                    out=sc["msq"][0], in0=sc["rq"][0], scalar1=1.0 / 2048),
                  reads=[sc["rq"][1]], writes=[sc["msq"][1]])
                E("vector", lambda e: e.tensor_mul(
                    out=sc["m2"][0], in0=sc["mean"][0], in1=sc["mean"][0]),
                  reads=[sc["mean"][1]], writes=[sc["m2"][1]])
                E("vector", lambda e: e.tensor_sub(
                    out=sc["var"][0], in0=sc["msq"][0], in1=sc["m2"][0]),
                  reads=[sc["msq"][1], sc["m2"][1]], writes=[sc["var"][1]])
                E("scalar", lambda e: e.activation(
                    out=sc["std"][0], in_=sc["var"][0], func=AF.Sqrt,
                    bias=epsv, scale=1.0),
                  reads=[sc["var"][1], epsB], writes=[sc["std"][1]])
                E("vector", lambda e: e.reciprocal(
                    out=sc["rstd"][0], in_=sc["std"][0]),
                  reads=[sc["std"][1]], writes=[sc["rstd"][1]])
                E("vector", lambda e: e.tensor_scalar(
                    out=sc["mrneg"][0], in0=sc["mean"][0],
                    scalar1=sc["rstd"][0], scalar2=-1.0,
                    op0=ALU.mult, op1=ALU.mult),
                  reads=[sc["mean"][1], sc["rstd"][1]], writes=[sc["mrneg"][1]])
                E("scalar", lambda e, x=xt: e.activation(
                    out=xln_tm, in_=x, func=AF.Identity,
                    bias=sc["mrneg"][0], scale=sc["rstd"][0]),
                  reads=[x_tmB[xb], sc["mrneg"][1], sc["rstd"][1]],
                  writes=[xln_tmB])
                for k in range(HC):
                    bk = k % 2
                    pv = banks[bk][:, 0:128]
                    E("tensor", lambda e, x=xt, k=k, pv=pv: e.transpose(
                        out=pv, in_=x[:, k * 128:(k + 1) * 128], identity=eyef),
                      reads=[x_tmB[xb], eyefB], writes=[bankB[bk]])
                    dst = A[:, k * T_PASS + ti * 128: k * T_PASS + ti * 128 + 128]
                    E("vector", lambda e, pv=pv, dst=dst: e.tensor_copy(
                        out=dst, in_=pv), reads=[bankB[bk]], writes=[Ab[k]])
                for k in range(HC):
                    bk = 2 + k % 2
                    pv = banks[bk].bitcast(BF16)[:, 0:128]
                    E("tensor", lambda e, k=k, pv=pv: e.transpose(
                        out=pv, in_=xln_tm[:, k * 128:(k + 1) * 128],
                        identity=eyeb),
                      reads=[xln_tmB, eyebB], writes=[bankB[bk]])
                    dst = xln_fm[:, k * 128:(k + 1) * 128]
                    E("vector", lambda e, pv=pv, dst=dst: e.tensor_copy(
                        out=dst, in_=pv), reads=[bankB[bk]], writes=[xln_fmB])
                qkv3v = qkv_tm.rearrange("p (s c d) -> p s c d", s=3, c=32)
                for p in range(16):
                    bk = 4 + p % 2
                    pv = banks[bk][:, 0:384]

                    def qkvmm(e, p=p, pv=pv):
                        e.matmul(out=pv, lhsT=xln_fm[:, p * 128:(p + 1) * 128],
                                 rhs=wqkvbd[:, p * 384:(p + 1) * 384],
                                 start=True, stop=False)
                        return e.matmul(out=pv, lhsT=onesrowb,
                                        rhs=biasrow[0:1, p * 384:(p + 1) * 384],
                                        start=False, stop=True)
                    E("tensor", qkvmm,
                      reads=[xln_fmB, wqkvbdB, onesrowbB, biasrowB],
                      writes=[bankB[bk]])
                    src = pv.rearrange("p (s c d) -> p s c d", s=3, c=2)
                    dst = qkv3v[:, :, 2 * p:2 * p + 2, :]
                    E("vector", lambda e, src=src, dst=dst: e.tensor_copy(
                        out=dst, in_=src), reads=[bankB[bk]], writes=[qkv_tmB])
                def st_smult(dc):
                    pb = dc % 2
                    d0 = dc * DCH
                    qv = qkv_tm.rearrange("p (s c d) -> p s d c", s=3, c=32)[
                        :, 0, d0:d0 + DCH, :].unsqueeze(3).broadcast_to(
                        [128, DCH, 32, 32])
                    kv = qkv_tm.rearrange("p (s c d) -> p s d c", s=3, c=32)[
                        :, 1, d0:d0 + DCH, :].unsqueeze(2).broadcast_to(
                        [128, DCH, 32, 32])
                    s_out = S2[pb].rearrange("p (d c e) -> p d c e",
                                             d=DCH, c=32)
                    E("vector", lambda e, qv=qv, kv=kv, s_out=s_out:
                      e.tensor_tensor(out=s_out, in0=qv, in1=kv, op=ALU.mult),
                      reads=[qkv_tmB], writes=[S2B[pb]])

                def st_exp(dc):
                    pb = dc % 2
                    E("scalar", lambda e, s=S2[pb], p=P2[pb]: e.activation(
                        out=p, in_=s, func=AF.Exp, scale=0.125),
                      reads=[S2B[pb]], writes=[P2B[pb]])

                def st_esum(dc):
                    pb = dc % 2
                    d0 = dc * DCH
                    pview = P2[pb].rearrange("p (d c e) -> p d c e",
                                             d=DCH, c=32)
                    E("vector", lambda e, pview=pview, o=sums2[pb]:
                      e.reduce_sum(out=o.rearrange("p (d c) -> p d c", d=DCH),
                                   in_=pview, axis=AX.X),
                      reads=[P2B[pb]], writes=[sums2B[pb]])
                    E("vector", lambda e, i_=sums2[pb], o=rc2[pb]:
                      e.reciprocal(out=o, in_=i_),
                      reads=[sums2B[pb]], writes=[rc2B[pb]])
                    vsl = qkv_tm.rearrange("p (s c d) -> p s c d", s=3, c=32)[
                        :, 2, :, d0:d0 + DCH]
                    rcv = rc2[pb].rearrange("p (d c) -> p c d", d=DCH)
                    E("vector", lambda e, vsl=vsl, rcv=rcv,
                      o=v2[pb].rearrange("p (c d) -> p c d", c=32):
                      e.tensor_tensor(out=o, in0=vsl, in1=rcv, op=ALU.mult),
                      reads=[qkv_tmB, rc2B[pb]], writes=[v2B[pb]])

                def st_gmul(dc):
                    pb = dc % 2
                    vbc = v2[pb].rearrange("p (c d) -> p d c", c=32).unsqueeze(
                        3).broadcast_to([128, DCH, 32, 32])
                    E("gpsimd", lambda e, p=P2[pb], vbc=vbc,
                      o=tmp2[pb].rearrange("p (d c m) -> p d c m", d=DCH, c=32):
                      e.tensor_tensor(out=o, in0=p.rearrange(
                          "p (d c m) -> p d c m", d=DCH, c=32), in1=vbc,
                          op=ALU.mult),
                      reads=[P2B[pb], v2B[pb]], writes=[tmp2B[pb]])

                def st_csum(dc):
                    pb = dc % 2
                    d0 = dc * DCH
                    tview = tmp2[pb].rearrange("p (d c m) -> p d m c",
                                               d=DCH, c=32)
                    cview = ctx.rearrange("p (m d) -> p d m", m=32)[
                        :, d0:d0 + DCH, :]
                    E("vector", lambda e, tview=tview, cview=cview:
                      e.reduce_sum(out=cview, in_=tview, axis=AX.X),
                      reads=[tmp2B[pb]], writes=[ctxB])

                for i in range(NDC + 2):
                    if i < NDC:
                        st_smult(i)
                        st_exp(i)
                    if 1 <= i and i - 1 < NDC:
                        st_esum(i - 1)
                        st_gmul(i - 1)
                    if 2 <= i:
                        st_csum(i - 2)
                if debug and h == 0 and ti == 0:
                    E("sync", lambda e: e.dma_start(out=dbgQ, in_=qkv_tm),
                      reads=[qkv_tmB], dma="dS")
                    E("sync", lambda e: e.dma_start(out=dbgX, in_=ctx),
                      reads=[ctxB], dma="dS")
                for j in range(16):
                    bk = 2 + j % 2
                    pv = banks[bk][:, 128:256]
                    E("tensor", lambda e, j=j, pv=pv: e.transpose(
                        out=pv, in_=ctx[:, j * 128:(j + 1) * 128],
                        identity=eyef),
                      reads=[ctxB, eyefB], writes=[bankB[bk]])
                    cf = ctx_fm[j % 2]
                    E("vector", lambda e, pv=pv, cf=cf: e.tensor_copy(
                        out=cf, in_=pv), reads=[bankB[bk]],
                      writes=[ctx_fmB[j % 2]])
                    bo = 6 + j % 2
                    po = banks[bo][:, 0:128]
                    E("tensor", lambda e, cf=cf, po=po: e.matmul(
                        out=po, lhsT=woutbd, rhs=cf, start=True, stop=True),
                      reads=[ctx_fmB[j % 2], woutbdB], writes=[bankB[bo]])
                    tb = t3[j % 2]
                    E("scalar", lambda e, po=po, tb=tb: e.activation(
                        out=tb, in_=po, func=AF.Identity, bias=bout2,
                        scale=1.0),
                      reads=[bankB[bo], bout2B], writes=[t3B[j % 2]])
                    asl = A[:, j * T_PASS + ti * 128: j * T_PASS + ti * 128 + 128]
                    E("vector", lambda e, asl=asl, tb=tb: e.tensor_add(
                        out=asl, in0=asl, in1=tb),
                      reads=[t3B[j % 2], Ab[j]], writes=[Ab[j]])
                if prelude_units:
                    ntake = -(-len(prelude_units) // (NT - ti))
                    for u in prelude_units[:ntake]:
                        pre_unit(*u)
                    del prelude_units[:ntake]

            if debug:
                E("sync", lambda e, h=h: e.dma_start(out=dbgA[h], in_=A),
                  reads=list(Ab), dma="dS")
            arena.reset()
            ln2x, _ln2 = arena.alloc("ln2x", 128, HC * T_PASS, BF16)
            ln2xB = [Buf(f"ln2x{k}") for k in range(HC)]
            abuf, _ab = arena.alloc("abuf", 128, ICE * T_PASS, BF16)
            abufB = [Buf(f"abuf{j}") for j in range(ICE)]
            wgb, wgbB = zip(*[arena.alloc(f"wgb{i}", 128, 2048, BF16)
                              for i in range(2)])
            wub, wubB = zip(*[arena.alloc(f"wub{i}", 128, 2048, BF16)
                              for i in range(2)])
            wdb, wdbB = zip(*[arena.alloc(f"wdb{i}", 128, 1408, BF16)
                              for i in range(2)])
            s_t, s_tB = zip(*[arena.alloc(f"s_t{i}", 128, 512, F32)
                              for i in range(2)])
            t5, t5B = zip(*[arena.alloc(f"t5_{i}", 128, 512, BF16)
                            for i in range(2)])
            C_b, C_bB = arena.alloc("C_b", 128, 1024, F32)
            R_b, R_bB = arena.alloc("R_b", 128, 512, F32)
            MR_b, MR_bB = arena.alloc("MR_b", 128, 512, F32)
            rows = {}
            for nm in ("mrow", "msqrow", "ra", "rb"):
                rows[nm] = arena.alloc("r_" + nm, 1, 512, F32)
            comb_all, comb_allB = arena.alloc("comb_all", 128, 128, F32)
            comb_fm, comb_fmB = arena.alloc("comb_fm", 16, 1024, BF16)
            tk = {}
            for nm in ("sorig", "swork", "msk", "acc4"):
                tk[nm] = arena.alloc("tk_" + nm, 128, 16, F32)
            for nm in ("mx", "nmx", "z4", "rz"):
                tk[nm] = arena.alloc("tk_" + nm, 128, 1, F32)

            for blk in range(NB):
                c0 = blk * 512

                def statmms(e, c0=c0):
                    last = None
                    for k in range(HC):
                        last = e.matmul(
                            out=banks[0][0:1, :], lhsT=onescol,
                            rhs=A[:, k * T_PASS + c0: k * T_PASS + c0 + 512],
                            start=(k == 0), stop=(k == HC - 1))
                    return last
                E("tensor", statmms, reads=list(Ab) + [onescolB],
                  writes=[bankB[0]])
                for k in range(HC):
                    sq = s_t[k % 2]
                    E("scalar", lambda e, k=k, sq=sq, c0=c0: e.activation(
                        out=sq, in_=A[:, k * T_PASS + c0: k * T_PASS + c0 + 512],
                        func=AF.Square),
                      reads=[Ab[k]], writes=[s_tB[k % 2]])
                    E("tensor", lambda e, sq=sq, k=k: e.matmul(
                        out=banks[1][0:1, :], lhsT=onescol, rhs=sq,
                        start=(k == 0), stop=(k == HC - 1)),
                      reads=[s_tB[k % 2], onescolB], writes=[bankB[1]])
                E("scalar", lambda e: e.activation(
                    out=rows["mrow"][0], in_=banks[0][0:1, :], func=AF.Identity,
                    bias=0.0, scale=1.0 / 2048),
                  reads=[bankB[0]], writes=[rows["mrow"][1]])
                E("scalar", lambda e: e.activation(
                    out=rows["msqrow"][0], in_=banks[1][0:1, :],
                    func=AF.Identity, bias=0.0, scale=1.0 / 2048),
                  reads=[bankB[1]], writes=[rows["msqrow"][1]])
                E("vector", lambda e: e.tensor_mul(
                    out=rows["ra"][0], in0=rows["mrow"][0], in1=rows["mrow"][0]),
                  reads=[rows["mrow"][1]], writes=[rows["ra"][1]])
                E("vector", lambda e: e.tensor_sub(
                    out=rows["rb"][0], in0=rows["msqrow"][0], in1=rows["ra"][0]),
                  reads=[rows["msqrow"][1], rows["ra"][1]],
                  writes=[rows["rb"][1]])
                E("scalar", lambda e: e.activation(
                    out=rows["ra"][0], in_=rows["rb"][0], func=AF.Sqrt,
                    bias=epsv[0:1, :], scale=1.0),
                  reads=[rows["rb"][1], epsB], writes=[rows["ra"][1]])
                E("vector", lambda e: e.reciprocal(
                    out=rows["rb"][0], in_=rows["ra"][0]),
                  reads=[rows["ra"][1]], writes=[rows["rb"][1]])
                E("vector", lambda e: e.tensor_mul(
                    out=rows["ra"][0], in0=rows["mrow"][0], in1=rows["rb"][0]),
                  reads=[rows["mrow"][1], rows["rb"][1]],
                  writes=[rows["ra"][1]])
                E("tensor", lambda e: e.matmul(
                    out=banks[2][:, :], lhsT=onesrowf, rhs=rows["rb"][0],
                    start=True, stop=True),
                  reads=[rows["rb"][1], onesrowfB], writes=[bankB[2]])
                E("tensor", lambda e: e.matmul(
                    out=banks[3][:, :], lhsT=onesrowf, rhs=rows["ra"][0],
                    start=True, stop=True),
                  reads=[rows["ra"][1], onesrowfB], writes=[bankB[3]])
                E("vector", lambda e: e.tensor_copy(out=R_b, in_=banks[2][:, :]),
                  reads=[bankB[2]], writes=[R_bB])
                E("vector", lambda e: e.tensor_copy(out=MR_b, in_=banks[3][:, :]),
                  reads=[bankB[3]], writes=[MR_bB])
                for k in range(HC):
                    sq = s_t[k % 2]
                    asl = A[:, k * T_PASS + c0: k * T_PASS + c0 + 512]
                    E("vector", lambda e, asl=asl, sq=sq: e.tensor_mul(
                        out=sq, in0=asl, in1=R_b),
                      reads=[Ab[k], R_bB], writes=[s_tB[k % 2]])
                    lsl = ln2x[:, k * T_PASS + c0: k * T_PASS + c0 + 512]
                    E("vector", lambda e, lsl=lsl, sq=sq: e.tensor_sub(
                        out=lsl, in0=sq, in1=MR_b),
                      reads=[s_tB[k % 2], MR_bB], writes=[ln2xB[k]])

            for ts in range(NT):
                def gatemms(e, ts=ts):
                    last = None
                    for k in range(HC):
                        last = e.matmul(
                            out=banks[4][:, 0:16],
                            lhsT=ln2x[:, k * T_PASS + ts * 128:
                                      k * T_PASS + ts * 128 + 128],
                            rhs=gtw[:, k * 16:(k + 1) * 16],
                            start=(k == 0), stop=(k == HC - 1))
                    return last
                E("tensor", gatemms, reads=list(ln2xB) + [gtwB],
                  writes=[bankB[4]])
                lg = banks[4][:, 0:16]
                E("vector", lambda e, lg=lg: e.reduce_max(
                    out=tk["mx"][0], in_=lg, axis=AX.X),
                  reads=[bankB[4]], writes=[tk["mx"][1]])
                E("vector", lambda e: e.tensor_scalar_mul(
                    out=tk["nmx"][0], in0=tk["mx"][0], scalar1=-1.0),
                  reads=[tk["mx"][1]], writes=[tk["nmx"][1]])
                E("scalar", lambda e, lg=lg: e.activation(
                    out=tk["sorig"][0], in_=lg, func=AF.Exp,
                    bias=tk["nmx"][0], scale=1.0),
                  reads=[bankB[4], tk["nmx"][1]], writes=[tk["sorig"][1]])
                E("vector", lambda e: e.tensor_copy(
                    out=tk["swork"][0], in_=tk["sorig"][0]),
                  reads=[tk["sorig"][1]], writes=[tk["swork"][1]])
                E("vector", lambda e: e.memset(tk["acc4"][0], 0.0),
                  writes=[tk["acc4"][1]])
                for it in range(4):
                    E("vector", lambda e: e.reduce_max(
                        out=tk["mx"][0], in_=tk["swork"][0], axis=AX.X),
                      reads=[tk["swork"][1]], writes=[tk["mx"][1]])
                    E("vector", lambda e: e.tensor_scalar(
                        out=tk["msk"][0], in0=tk["swork"][0],
                        scalar1=tk["mx"][0], scalar2=None, op0=ALU.is_equal),
                      reads=[tk["swork"][1], tk["mx"][1]], writes=[tk["msk"][1]])
                    E("vector", lambda e: e.scalar_tensor_tensor(
                        out=tk["swork"][0], in0=tk["msk"][0], scalar=-1e30,
                        in1=tk["swork"][0], op0=ALU.mult, op1=ALU.add),
                      reads=[tk["msk"][1], tk["swork"][1]],
                      writes=[tk["swork"][1]])
                    E("vector", lambda e: e.tensor_add(
                        out=tk["acc4"][0], in0=tk["acc4"][0], in1=tk["msk"][0]),
                      reads=[tk["msk"][1], tk["acc4"][1]],
                      writes=[tk["acc4"][1]])
                E("vector", lambda e: e.tensor_mul(
                    out=tk["msk"][0], in0=tk["acc4"][0], in1=tk["sorig"][0]),
                  reads=[tk["acc4"][1], tk["sorig"][1]], writes=[tk["msk"][1]])
                E("vector", lambda e: e.reduce_sum(
                    out=tk["z4"][0], in_=tk["msk"][0], axis=AX.X),
                  reads=[tk["msk"][1]], writes=[tk["z4"][1]])
                E("vector", lambda e: e.reciprocal(
                    out=tk["rz"][0], in_=tk["z4"][0]),
                  reads=[tk["z4"][1]], writes=[tk["rz"][1]])
                E("vector", lambda e, ts=ts: e.tensor_scalar_mul(
                    out=comb_all[:, ts * 16:(ts + 1) * 16], in0=tk["msk"][0],
                    scalar1=tk["rz"][0]),
                  reads=[tk["msk"][1], tk["rz"][1]], writes=[comb_allB])
            for ts in range(NT):
                pv = banks[5][0:16, 0:128]
                E("tensor", lambda e, ts=ts, pv=pv: e.transpose(
                    out=pv, in_=comb_all[:, ts * 16:(ts + 1) * 16],
                    identity=eyef),
                  reads=[comb_allB, eyefB], writes=[bankB[5]])
                E("vector", lambda e, ts=ts, pv=pv: e.tensor_copy(
                    out=comb_fm[:, ts * 128:(ts + 1) * 128], in_=pv),
                  reads=[bankB[5]], writes=[comb_fmB])

            if debug:
                E("sync", lambda e, h=h, l=ln2x: e.dma_start(out=dbgL[h], in_=l),
                  reads=list(ln2xB), dma="dS")
                E("sync", lambda e, h=h, c=comb_all: e.dma_start(out=dbgC[h], in_=c),
                  reads=[comb_allB], dma="dS")
            for ex in range(NEXP + 2):
                shared = ex >= NEXP
                ioff = 0 if ex != NEXP + 1 else 11
                joff = 0 if ex != NEXP + 1 else 1408
                if not shared:
                    for blk in range(NB):
                        E("tensor", lambda e, ex=ex, blk=blk: e.matmul(
                            out=banks[6][:, :],
                            lhsT=sel[:, ex * 128:(ex + 1) * 128],
                            rhs=comb_fm[:, blk * 512:(blk + 1) * 512],
                            start=True, stop=True),
                          reads=[selB, comb_fmB], writes=[bankB[6]])
                        E("vector", lambda e, blk=blk: e.tensor_copy(
                            out=C_b[:, blk * 512:(blk + 1) * 512],
                            in_=banks[6][:, :]),
                          reads=[bankB[6]], writes=[C_bB])
                for isl in range(ICE):
                    par = isl % 2
                    E("sync", lambda e, s=wgsc_d[ex][isl], d=wgb[par]:
                      e.dma_start(out=d, in_=s),
                      reads=[wgscB[ex]], writes=[wgbB[par]], dma="dS")
                    E("scalar", lambda e, s=wusc_d[ex][isl], d=wub[par]:
                      e.dma_start(out=d, in_=s),
                      reads=[wuscB[ex]], writes=[wubB[par]], dma="dA")
                    gb0, gb1 = 4 * par, 4 * par + 1
                    ub0, ub1 = 4 * par + 2, 4 * par + 3

                    def gumms(e, par=par):
                        last = None
                        for k in range(HC):
                            st_, sp = (k == 0), (k == HC - 1)
                            for blk in range(NB):
                                gbk = banks[4 * par + blk]
                                ubk = banks[4 * par + 2 + blk]
                                e.matmul(
                                    out=gbk[:, :],
                                    lhsT=wgb[par][:, k * 128:(k + 1) * 128],
                                    rhs=ln2x[:, k * T_PASS + blk * 512:
                                             k * T_PASS + blk * 512 + 512],
                                    start=st_, stop=sp)
                                last = e.matmul(
                                    out=ubk[:, :],
                                    lhsT=wub[par][:, k * 128:(k + 1) * 128],
                                    rhs=ln2x[:, k * T_PASS + blk * 512:
                                             k * T_PASS + blk * 512 + 512],
                                    start=st_, stop=sp)
                        return last
                    E("tensor", gumms,
                      reads=[wgbB[par], wubB[par]] + list(ln2xB),
                      writes=[bankB[gb0], bankB[gb1], bankB[ub0], bankB[ub1]])
                    for blk in range(NB):
                        gbk = banks[4 * par + blk]
                        ubk = banks[4 * par + 2 + blk]
                        gB, uB = bankB[4 * par + blk], bankB[4 * par + 2 + blk]
                        stv = s_t[blk]
                        E("scalar", lambda e, gbk=gbk, stv=stv: e.activation(
                            out=stv, in_=gbk[:, :], func=AF.Silu),
                          reads=[gB], writes=[s_tB[blk]])
                        aslice = abuf[:, isl * T_PASS + blk * 512:
                                      isl * T_PASS + blk * 512 + 512]
                        if shared:
                            E("vector", lambda e, stv=stv, ubk=ubk,
                              aslice=aslice: e.tensor_mul(
                                  out=aslice, in0=stv, in1=ubk[:, :]),
                              reads=[s_tB[blk], uB], writes=[abufB[isl]])
                        else:
                            tv = t5[blk]
                            E("vector", lambda e, stv=stv, ubk=ubk, tv=tv:
                              e.tensor_mul(out=tv, in0=stv, in1=ubk[:, :]),
                              reads=[s_tB[blk], uB], writes=[t5B[blk]])
                            E("gpsimd", lambda e, tv=tv, aslice=aslice,
                              blk=blk: e.tensor_tensor(
                                  out=aslice, in0=tv,
                                  in1=C_b[:, blk * 512:(blk + 1) * 512],
                                  op=ALU.mult),
                              reads=[t5B[blk], C_bB], writes=[abufB[isl]])
                for hc in range(HC):
                    par = hc % 2
                    E("sync", lambda e, s=wdsc_d[ex][hc], d=wdb[par]:
                      e.dma_start(out=d, in_=s),
                      reads=[wdscB[ex]], writes=[wdbB[par]], dma="dS")
                    yb0, yb1 = 4 + 2 * par, 5 + 2 * par

                    def dmms(e, par=par):
                        last = None
                        for j in range(ICE):
                            st_, sp = (j == 0), (j == ICE - 1)
                            for blk in range(NB):
                                ybk = banks[4 + 2 * par + blk]
                                last = e.matmul(
                                    out=ybk[:, :],
                                    lhsT=wdb[par][:, j * 128:(j + 1) * 128],
                                    rhs=abuf[:, j * T_PASS + blk * 512:
                                             j * T_PASS + blk * 512 + 512],
                                    start=st_, stop=sp)
                        return last
                    E("tensor", dmms, reads=[wdbB[par]] + list(abufB),
                      writes=[bankB[yb0], bankB[yb1]])
                    for blk in range(NB):
                        ybk = banks[4 + 2 * par + blk]
                        yB = bankB[4 + 2 * par + blk]
                        asl = A[:, hc * T_PASS + blk * 512:
                                hc * T_PASS + blk * 512 + 512]
                        E("vector", lambda e, asl=asl, ybk=ybk: e.tensor_add(
                            out=asl, in0=asl, in1=ybk[:, :]),
                          reads=[yB, Ab[hc]], writes=[Ab[hc]])

            arena.reset()
            out_tm, out_tmB = zip(*[arena.alloc(f"out_tm{i}", 128, 2048, F32)
                                    for i in range(2)])
            for ti in range(NT):
                ob = ti % 2
                for k in range(HC):
                    bk = k % 2
                    pv = banks[bk][:, 0:128]
                    E("tensor", lambda e, k=k, ti=ti, pv=pv: e.transpose(
                        out=pv,
                        in_=A[:, k * T_PASS + ti * 128: k * T_PASS + ti * 128 + 128],
                        identity=eyef),
                      reads=[Ab[k], eyefB], writes=[bankB[bk]])
                    E("vector", lambda e, k=k, pv=pv, d=out_tm[ob]:
                      e.tensor_copy(out=d[:, k * 128:(k + 1) * 128], in_=pv),
                      reads=[bankB[bk]], writes=[out_tmB[ob]])
                E("sync", lambda e, d=out_tm[ob], r0=tok0 + ti * 128:
                  e.dma_start(out=out_d[r0:r0 + 128, :], in_=d),
                  reads=[out_tmB[ob]], dma="dS")

        plan.wait_only("sync", [("dS", plan.cnt.get("dS", 0)),
                                ("dA", plan.cnt.get("dA", 0))])

        def run_ops(handle, eng):
            for fn, waits, sem, amt in plan.ops[eng]:
                for s, v in waits:
                    handle.wait_ge(sems[s], v)
                if fn is not None:
                    ins = fn(handle)
                    if sem is not None:
                        ins.then_inc(sems[sem], amt)

        @block.sync
        def _(e):
            run_ops(e, "sync")

        @block.vector
        def _(e):
            run_ops(e, "vector")

        @block.scalar
        def _(e):
            run_ops(e, "scalar")

        @block.tensor
        def _(e):
            run_ops(e, "tensor")

        @block.gpsimd
        def _(e):
            run_ops(e, "gpsimd")

    return nc


def host_prepare(inputs):
    Wqkv = np.asarray(inputs["L0_attn_Wqkv"], dtype=np.float32)
    bqkv = np.asarray(inputs["L0_attn_bqkv"], dtype=np.float32)
    Wout = np.asarray(inputs["L0_attn_Wout"], dtype=np.float32)
    bout = np.asarray(inputs["L0_attn_bout"], dtype=np.float32)
    gate_w = np.asarray(inputs["L0_ffn_gate_w"], dtype=np.float32)

    wbd = np.zeros((16, 2, 64, 3, 2, 64), dtype=np.float32)
    wq5 = Wqkv.reshape(16, 2, 64, 3, 64)
    for c2 in range(2):
        wbd[:, c2, :, :, c2, :] = wq5[:, c2]
    wqkv_bd = wbd.reshape(16, 128, 384)

    bias_pairs = bqkv.reshape(16, 2, 3, 64).transpose(0, 2, 1, 3).reshape(1, 6144)

    wout_bd = np.zeros((128, 128), dtype=np.float32)
    wout_bd[0:64, 0:64] = Wout
    wout_bd[64:128, 64:128] = Wout

    bout2 = np.concatenate([bout, bout]).reshape(128, 1)
    gate_wT = np.ascontiguousarray(gate_w.T)
    eye128 = np.eye(128, dtype=np.float32)

    return {
        "w_gate": np.ascontiguousarray(inputs["L0_ffn_w_gate"], dtype=np.float32),
        "w_up": np.ascontiguousarray(inputs["L0_ffn_w_up"], dtype=np.float32),
        "w_down": np.ascontiguousarray(inputs["L0_ffn_w_down"], dtype=np.float32),
        "sw_gate": np.ascontiguousarray(inputs["L0_ffn_sw_gate"], dtype=np.float32),
        "sw_up": np.ascontiguousarray(inputs["L0_ffn_sw_up"], dtype=np.float32),
        "sw_down": np.ascontiguousarray(inputs["L0_ffn_sw_down"], dtype=np.float32),
        "wqkv_bd": wqkv_bd,
        "bias_pairs": bias_pairs,
        "wout_bd": wout_bd,
        "bout2": bout2,
        "gate_wT": gate_wT,
        "eye128": eye128,
    }


def kernel(**inputs):
    x = np.asarray(inputs["L0_attn_x"], dtype=np.float32).reshape(16384, 2048)
    shared = host_prepare(inputs)
    nc = build_program()
    in_maps = []
    for c in range(N_CORES):
        m = dict(shared)
        m["xin"] = np.ascontiguousarray(x[c * T_CORE:(c + 1) * T_CORE, :])
        in_maps.append(m)
    res = run_bass_kernel_spmd(nc, in_maps, core_ids=list(range(N_CORES)))
    out = np.concatenate([r["out"] for r in res.results], axis=0)
    return out.reshape(1, 16384, 2048)


if __name__ == "__main__":
    nc = build_program()
    print("build ok")


# version 06
# speedup vs baseline: 1.0041x; 1.0041x over previous
import numpy as np
import concourse.bass as bass
import concourse.mybir as mybir
from concourse.bass_utils import run_bass_kernel_spmd

F32 = mybir.dt.float32
BF16 = mybir.dt.bfloat16
U8 = mybir.dt.uint8
AF = mybir.ActivationFunctionType
ALU = mybir.AluOpType
AX = mybir.AxisListType

N_CORES = 8
T_CORE = 2048
N_PASS = 2
T_PASS = 1024
NT = T_PASS // 128
NB = T_PASS // 512
HC = 16
ICE = 11
NEXP = 16
EPS = 1e-5
DCH = 1
NDC = 64 // DCH

ENG = ("sync", "vector", "scalar", "tensor", "gpsimd")
ESEM = {"vector": "v", "scalar": "s", "tensor": "t", "gpsimd": "g"}


class Buf:
    __slots__ = ("name", "w", "readers")

    def __init__(self, name):
        self.name = name
        self.w = None
        self.readers = []


class Plan:
    def __init__(self):
        self.ops = {e: [] for e in ENG}
        self.cnt = {}
        self.known = {e: {} for e in ENG}

    def emit(self, eng, fn, reads=(), writes=(), dma=None, extra=()):
        waits = {}

        def need(sw):
            if sw is None:
                return
            s, v = sw
            if self.known[eng].get(s, 0) >= v:
                return
            if waits.get(s, 0) < v:
                waits[s] = v

        for b in reads:
            need(b.w)
        for b in writes:
            need(b.w)
            for sw in b.readers:
                need(sw)
        for sw in extra:
            need(sw)
        sem = dma if dma else ESEM[eng]
        amt = 16 if dma else 1
        self.cnt[sem] = self.cnt.get(sem, 0) + amt
        v = self.cnt[sem]
        for s, w in waits.items():
            self.known[eng][s] = max(self.known[eng].get(s, 0), w)
        self.ops[eng].append((fn, sorted(waits.items()), sem, amt))
        for b in writes:
            b.w = (sem, v)
            b.readers = []
        for b in reads:
            b.readers.append((sem, v))
        return (sem, v)

    def wait_only(self, eng, pairs):
        waits = {}
        for s, v in pairs:
            if self.known[eng].get(s, 0) >= v:
                continue
            waits[s] = max(waits.get(s, 0), v)
        if not waits:
            return
        for s, w in waits.items():
            self.known[eng][s] = max(self.known[eng].get(s, 0), w)
        self.ops[eng].append((None, sorted(waits.items()), None, 0))


class Arena:
    SIZES = {F32: 4, BF16: 2, U8: 1}

    def __init__(self, ap, nbytes):
        self.base = ap
        self.nbytes = nbytes
        self.off = 0
        self.ghost = {}
        self.live = []
        self.peak = 0

    def reset(self):
        deps = dict(self.ghost)
        for b in self.live:
            sws = ([b.w] if b.w else []) + b.readers
            for s, v in sws:
                deps[s] = max(deps.get(s, 0), v)
        self.ghost = deps
        self.live = []
        self.peak = max(self.peak, self.off)
        self.off = 0

    def alloc(self, name, parts, n, dtype):
        nbytes = n * self.SIZES[dtype]
        off = (self.off + 3) & ~3
        assert off + nbytes <= self.nbytes, (
            f"arena overflow at {name}: {off + nbytes} > {self.nbytes}")
        self.off = off + nbytes
        view = self.base[0:parts, off:off + nbytes].bitcast(dtype)
        b = Buf(name)
        if self.ghost:
            b.readers = list(self.ghost.items())
        self.live.append(b)
        return view, b


def build_program(debug=False):
    nc = bass.Bass("TRN2", target_bir_lowering=False)

    xin = nc.dram_tensor("xin", [T_CORE, 2048], F32, kind="ExternalInput").ap()
    wg_d = nc.dram_tensor("w_gate", [16, 2048, 1408], F32, kind="ExternalInput").ap()
    wu_d = nc.dram_tensor("w_up", [16, 2048, 1408], F32, kind="ExternalInput").ap()
    wd_d = nc.dram_tensor("w_down", [16, 1408, 2048], F32, kind="ExternalInput").ap()
    swg_d = nc.dram_tensor("sw_gate", [2048, 2816], F32, kind="ExternalInput").ap()
    swu_d = nc.dram_tensor("sw_up", [2048, 2816], F32, kind="ExternalInput").ap()
    swd_d = nc.dram_tensor("sw_down", [2816, 2048], F32, kind="ExternalInput").ap()
    wqkvbd_d = nc.dram_tensor("wqkv_bd", [16, 128, 384], F32, kind="ExternalInput").ap()
    biasp_d = nc.dram_tensor("bias_pairs", [1, 6144], F32, kind="ExternalInput").ap()
    woutbd_d = nc.dram_tensor("wout_bd", [128, 128], F32, kind="ExternalInput").ap()
    bout2_d = nc.dram_tensor("bout2", [128, 1], F32, kind="ExternalInput").ap()
    gwt_d = nc.dram_tensor("gate_wT", [2048, 16], F32, kind="ExternalInput").ap()
    eye_d = nc.dram_tensor("eye128", [128, 128], F32, kind="ExternalInput").ap()
    out_d = nc.dram_tensor("out", [T_CORE, 2048], F32, kind="ExternalOutput").ap()
    wgsc_d = nc.dram_tensor("wgsc", [18, 11, 128, 2048], BF16, kind="Internal").ap()
    wusc_d = nc.dram_tensor("wusc", [18, 11, 128, 2048], BF16, kind="Internal").ap()
    wdsc_d = nc.dram_tensor("wdsc", [18, 16, 128, 1408], BF16, kind="Internal").ap()
    if debug:
        dbgA = [nc.dram_tensor(f"dbgA{h}", [128, 16384], F32,
                               kind="ExternalOutput").ap() for h in range(2)]
        dbgL = [nc.dram_tensor(f"dbgL{h}", [128, 16384], BF16,
                               kind="ExternalOutput").ap() for h in range(2)]
        dbgC = [nc.dram_tensor(f"dbgC{h}", [128, 128], F32,
                               kind="ExternalOutput").ap() for h in range(2)]
        dbgQ = nc.dram_tensor("dbgQ", [128, 6144], BF16,
                              kind="ExternalOutput").ap()
        dbgX = nc.dram_tensor("dbgX", [128, 2048], F32,
                              kind="ExternalOutput").ap()

    plan = Plan()
    from contextlib import ExitStack
    with ExitStack() as st:
        def sbt(name, shape, dtype):
            return st.enter_context(nc.sbuf_tensor(name, shape, dtype))

        A_t = sbt("A", [128, HC * T_PASS], F32)
        wqkvbd_t = sbt("wqkvbd", [128, 6144], BF16)
        woutbd_t = sbt("woutbd", [128, 128], BF16)
        bout2_t = sbt("bout2_sb", [128, 1], F32)
        gtw_t = sbt("gtw", [128, 256], BF16)
        sel_t = sbt("sel", [16, 2048], BF16)
        eyef_t = sbt("eyef", [128, 128], F32)
        eyeb_t = sbt("eyeb", [128, 128], BF16)
        onescol_t = sbt("onescol", [128, 1], F32)
        onesrowf_t = sbt("onesrowf", [1, 128], F32)
        onesrowb_t = sbt("onesrowb", [1, 128], BF16)
        eps_t = sbt("eps_sb", [128, 1], F32)
        ARENA_BYTES = 116256
        arena_t = sbt("arena", [128, ARENA_BYTES], U8)

        psum_t = [st.enter_context(nc.psum_tensor(f"bank{i}", [128, 512], F32))
                  for i in range(8)]
        banks = [p.ap() for p in psum_t]
        bankB = [Buf(f"bank{i}") for i in range(8)]

        sem_names = ["v", "s", "t", "g", "dC", "dDBG", "dX0", "dX1",
                     "dG0", "dG1", "dU0", "dU1", "dD0", "dD1",
                     "dPF0", "dPF1", "dPW0", "dPW1", "dO0", "dO1"]
        sems = {n: st.enter_context(nc.semaphore("sem_" + n)) for n in sem_names}
        block = st.enter_context(nc.Block())

        arena = Arena(arena_t.ap(), ARENA_BYTES)

        A = A_t.ap()
        Ab = [Buf(f"A{k}") for k in range(HC)]
        wgscB = [Buf(f"wgsc{e}") for e in range(NEXP + 2)]
        wuscB = [Buf(f"wusc{e}") for e in range(NEXP + 2)]
        wdscB = [Buf(f"wdsc{e}") for e in range(NEXP + 2)]
        wqkvbd = wqkvbd_t.ap(); wqkvbdB = Buf("wqkvbd")
        woutbd = woutbd_t.ap(); woutbdB = Buf("woutbd")
        bout2 = bout2_t.ap(); bout2B = Buf("bout2")
        gtw = gtw_t.ap(); gtwB = Buf("gtw")
        sel = sel_t.ap(); selB = Buf("sel")
        eyef = eyef_t.ap(); eyefB = Buf("eyef")
        eyeb = eyeb_t.ap(); eyebB = Buf("eyeb")
        onescol = onescol_t.ap(); onescolB = Buf("onescol")
        onesrowf = onesrowf_t.ap(); onesrowfB = Buf("onesrowf")
        onesrowb = onesrowb_t.ap(); onesrowbB = Buf("onesrowb")
        epsv = eps_t.ap(); epsB = Buf("eps")

        E = plan.emit

        def chain(sem):
            return [(sem, plan.cnt.get(sem, 0))]

        stg, stgB = arena.alloc("stg", 128, 6144, F32)
        E("sync", lambda e, d=stg: e.dma_start(out=d, in_=wqkvbd_d.rearrange(
            "k p i -> p k i")), writes=[stgB], dma="dC", extra=chain("dC"))
        E("vector", lambda e, s=stg: e.tensor_copy(out=wqkvbd, in_=s),
          reads=[stgB], writes=[wqkvbdB])
        stg3, stg3B = arena.alloc("stg3", 128, 128, F32)
        E("sync", lambda e, d=stg3: e.dma_start(out=d, in_=woutbd_d),
          writes=[stg3B], dma="dC", extra=chain("dC"))
        E("vector", lambda e, s=stg3: e.tensor_copy(out=woutbd, in_=s),
          reads=[stg3B], writes=[woutbdB])
        E("sync", lambda e: e.dma_start(out=bout2, in_=bout2_d),
          writes=[bout2B], dma="dC", extra=chain("dC"))
        stg4, stg4B = arena.alloc("stg4", 128, 256, F32)
        E("sync", lambda e, d=stg4: e.dma_start(out=d, in_=gwt_d.rearrange(
            "(k p) e -> p k e", p=128)), writes=[stg4B], dma="dC", extra=chain("dC"))
        E("vector", lambda e, s=stg4: e.tensor_copy(out=gtw, in_=s),
          reads=[stg4B], writes=[gtwB])
        E("sync", lambda e: e.dma_start(out=eyef, in_=eye_d),
          writes=[eyefB], dma="dC", extra=chain("dC"))
        E("vector", lambda e: e.tensor_copy(out=eyeb, in_=eyef),
          reads=[eyefB], writes=[eyebB])
        E("vector", lambda e: e.memset(onescol, 1.0), writes=[onescolB])
        E("vector", lambda e: e.memset(epsv, EPS), writes=[epsB])
        E("vector", lambda e: e.memset(onesrowf, 1.0), writes=[onesrowfB])
        E("vector", lambda e: e.memset(onesrowb, 1.0), writes=[onesrowbB])
        ioe, ioeB = arena.alloc("ioe", 16, 2048, F32)
        iop, iopB = arena.alloc("iop", 16, 2048, F32)
        E("gpsimd", lambda e, d=ioe: e.iota(out=d, pattern=[[1, 16], [0, 128]],
          base=0, channel_multiplier=0, allow_small_or_imprecise_dtypes=True),
          writes=[ioeB])
        E("gpsimd", lambda e, d=iop: e.iota(out=d, pattern=[[0, 16], [0, 128]],
          base=0, channel_multiplier=1, allow_small_or_imprecise_dtypes=True),
          writes=[iopB])
        E("vector", lambda e: e.tensor_tensor(out=sel, in0=ioe, in1=iop,
          op=ALU.is_equal), reads=[ioeB, iopB], writes=[selB])

        qkv3 = None

        for h in range(N_PASS):
            tok0 = h * T_PASS
            arena.reset()
            biasrow, biasrowB = arena.alloc("biasrow", 1, 6144, BF16)
            bstg, bstgB = arena.alloc("bstg", 1, 512, F32)
            for bc in range(12):
                E("sync", lambda e, d=bstg, bc=bc: e.dma_start(
                    out=d, in_=biasp_d[:, bc * 512:(bc + 1) * 512]),
                  writes=[bstgB], dma="dC", extra=chain("dC"))
                E("vector", lambda e, s=bstg, bc=bc: e.tensor_copy(
                    out=biasrow[:, bc * 512:(bc + 1) * 512], in_=s),
                  reads=[bstgB], writes=[biasrowB])
            x_tm, x_tmB = zip(*[arena.alloc(f"x_tm{i}", 128, 2048, F32)
                                for i in range(2)])
            qkv_tm, qkv_tmB = arena.alloc("qkv_tm", 128, 6144, BF16)
            xln_tm, xln_tmB = arena.alloc("xln_tm", 128, 2048, BF16)
            xln_fm, xln_fmB = arena.alloc("xln_fm", 128, 2048, BF16)
            S2, S2B = zip(*[arena.alloc(f"S2_{i}", 128, 1024 * DCH, BF16)
                            for i in range(2)])
            P2, P2B = zip(*[arena.alloc(f"P2_{i}", 128, 1024 * DCH, BF16)
                            for i in range(2)])
            tmp2, tmp2B = zip(*[arena.alloc(f"tmp2_{i}", 128, 1024 * DCH, BF16)
                                for i in range(2)])
            sums2, sums2B = zip(*[arena.alloc(f"sums2_{i}", 128, 32 * DCH, F32)
                                  for i in range(2)])
            rc2, rc2B = zip(*[arena.alloc(f"rc2_{i}", 128, 32 * DCH, F32)
                              for i in range(2)])
            v2, v2B = zip(*[arena.alloc(f"v2_{i}", 128, 32 * DCH, BF16)
                            for i in range(2)])
            ctx, ctxB = arena.alloc("ctx", 128, 2048, F32)
            xsq, xsqB = ctx, ctxB
            ctx_fm, ctx_fmB = zip(*[arena.alloc(f"ctxfm{i}", 128, 128, BF16)
                                    for i in range(2)])
            t3, t3B = zip(*[arena.alloc(f"t3_{i}", 128, 128, F32)
                            for i in range(1)]); t3 = t3 * 2; t3B = t3B * 2
            sc = {}
            for nm in ("rs", "rq", "mean", "msq", "m2", "var", "std", "rstd",
                       "mrneg"):
                sc[nm] = arena.alloc("ln1_" + nm, 128, 1, F32)

            prelude_units = []
            if h == 0:
                pf, pfB = zip(*[arena.alloc(f"pf{i}", 128, 2048, F32)
                                for i in range(2)])
                pbv, pbB = zip(*[arena.alloc(f"pb{i}", 128, 2048, BF16)
                                 for i in range(2)])
                pstate = {"n": 0}

                def pre_unit(kind, ex, idx):
                    par = pstate["n"] % 2
                    pstate["n"] += 1
                    ioff = 0 if ex != NEXP + 1 else 11
                    joff = 0 if ex != NEXP + 1 else 1408
                    if kind == "g":
                        if ex < NEXP:
                            srcv = wg_d[ex][idx * 128:(idx + 1) * 128, :]
                        else:
                            srcv = swg_d[idx * 128:(idx + 1) * 128,
                                         ioff * 128:ioff * 128 + 1408]
                        n = 1408
                        eng = "scalar"
                        dst = wgsc_d[ex][:, :, idx * 128:(idx + 1) * 128]
                    elif kind == "u":
                        if ex < NEXP:
                            srcv = wu_d[ex][idx * 128:(idx + 1) * 128, :]
                        else:
                            srcv = swu_d[idx * 128:(idx + 1) * 128,
                                         ioff * 128:ioff * 128 + 1408]
                        n = 1408
                        eng = "gpsimd"
                        dst = wusc_d[ex][:, :, idx * 128:(idx + 1) * 128]
                    else:
                        if ex < NEXP:
                            srcv = wd_d[ex][idx * 128:(idx + 1) * 128, :]
                        else:
                            srcv = swd_d[joff + idx * 128:
                                         joff + (idx + 1) * 128, :]
                        n = 2048
                        eng = "scalar"
                        dst = wdsc_d[ex][:, :, idx * 128:(idx + 1) * 128]
                    fv = pf[par][:, 0:n]
                    bv = pbv[par][:, 0:n]
                    E("sync", lambda e, s=srcv, d=fv: e.dma_start(
                        out=d, in_=s), writes=[pfB[par]], dma=f"dPF{par}")
                    if eng == "scalar":
                        E("scalar", lambda e, s=fv, d=bv: e.activation(
                            out=d, in_=s, func=AF.Identity),
                          reads=[pfB[par]], writes=[pbB[par]])
                    else:
                        E("gpsimd", lambda e, s=fv, d=bv: e.tensor_copy(
                            out=d, in_=s), reads=[pfB[par]], writes=[pbB[par]])
                    if kind == "d":
                        sview = bv.rearrange("p (hh x) -> p hh x", hh=16)
                        dview = dst.rearrange("hh p x -> p hh x")
                        wb = wdscB[ex]
                    else:
                        sview = bv.rearrange("p (i x) -> p i x", i=11)
                        dview = dst.rearrange("i p x -> p i x")
                        wb = wgscB[ex] if kind == "g" else wuscB[ex]
                    E("scalar", lambda e, s=sview, d=dview: e.dma_start(
                        out=d, in_=s), reads=[pbB[par]],
                      writes=[wb], dma=f"dPW{par}")

                for ex in range(NEXP + 2):
                    for k in range(HC):
                        prelude_units.append(("g", ex, k))
                        prelude_units.append(("u", ex, k))
                    for j in range(ICE):
                        prelude_units.append(("d", ex, j))

            for ti in range(NT):
                xb = ti % 2
                xt = x_tm[xb]
                E("sync", lambda e, d=xt, r0=tok0 + ti * 128:
                  e.dma_start(out=d, in_=xin[r0:r0 + 128, :]),
                  writes=[x_tmB[xb]], dma=f"dX{xb}")
                E("vector", lambda e, x=xt: e.reduce_sum(
                    out=sc["rs"][0], in_=x, axis=AX.X),
                  reads=[x_tmB[xb]], writes=[sc["rs"][1]])
                E("vector", lambda e, x=xt, s=xsq: e.scalar_tensor_tensor(
                    out=s, in0=x, scalar=1.0, in1=x, op0=ALU.mult,
                    op1=ALU.mult, accum_out=sc["rq"][0]),
                  reads=[x_tmB[xb]], writes=[xsqB, sc["rq"][1]])
                E("vector", lambda e: e.tensor_scalar_mul(
                    out=sc["mean"][0], in0=sc["rs"][0], scalar1=1.0 / 2048),
                  reads=[sc["rs"][1]], writes=[sc["mean"][1]])
                E("vector", lambda e: e.tensor_scalar_mul(
                    out=sc["msq"][0], in0=sc["rq"][0], scalar1=1.0 / 2048),
                  reads=[sc["rq"][1]], writes=[sc["msq"][1]])
                E("vector", lambda e: e.tensor_mul(
                    out=sc["m2"][0], in0=sc["mean"][0], in1=sc["mean"][0]),
                  reads=[sc["mean"][1]], writes=[sc["m2"][1]])
                E("vector", lambda e: e.tensor_sub(
                    out=sc["var"][0], in0=sc["msq"][0], in1=sc["m2"][0]),
                  reads=[sc["msq"][1], sc["m2"][1]], writes=[sc["var"][1]])
                E("scalar", lambda e: e.activation(
                    out=sc["std"][0], in_=sc["var"][0], func=AF.Sqrt,
                    bias=epsv, scale=1.0),
                  reads=[sc["var"][1], epsB], writes=[sc["std"][1]])
                E("vector", lambda e: e.reciprocal(
                    out=sc["rstd"][0], in_=sc["std"][0]),
                  reads=[sc["std"][1]], writes=[sc["rstd"][1]])
                E("vector", lambda e: e.tensor_scalar(
                    out=sc["mrneg"][0], in0=sc["mean"][0],
                    scalar1=sc["rstd"][0], scalar2=-1.0,
                    op0=ALU.mult, op1=ALU.mult),
                  reads=[sc["mean"][1], sc["rstd"][1]], writes=[sc["mrneg"][1]])
                E("scalar", lambda e, x=xt: e.activation(
                    out=xln_tm, in_=x, func=AF.Identity,
                    bias=sc["mrneg"][0], scale=sc["rstd"][0]),
                  reads=[x_tmB[xb], sc["mrneg"][1], sc["rstd"][1]],
                  writes=[xln_tmB])
                for k in range(HC):
                    bk = k % 2
                    pv = banks[bk][:, 0:128]
                    E("tensor", lambda e, x=xt, k=k, pv=pv: e.transpose(
                        out=pv, in_=x[:, k * 128:(k + 1) * 128], identity=eyef),
                      reads=[x_tmB[xb], eyefB], writes=[bankB[bk]])
                    dst = A[:, k * T_PASS + ti * 128: k * T_PASS + ti * 128 + 128]
                    E("vector", lambda e, pv=pv, dst=dst: e.tensor_copy(
                        out=dst, in_=pv), reads=[bankB[bk]], writes=[Ab[k]])
                for k in range(HC):
                    bk = 2 + k % 2
                    pv = banks[bk].bitcast(BF16)[:, 0:128]
                    E("tensor", lambda e, k=k, pv=pv: e.transpose(
                        out=pv, in_=xln_tm[:, k * 128:(k + 1) * 128],
                        identity=eyeb),
                      reads=[xln_tmB, eyebB], writes=[bankB[bk]])
                    dst = xln_fm[:, k * 128:(k + 1) * 128]
                    E("vector", lambda e, pv=pv, dst=dst: e.tensor_copy(
                        out=dst, in_=pv), reads=[bankB[bk]], writes=[xln_fmB])
                qkv3v = qkv_tm.rearrange("p (s c d) -> p s c d", s=3, c=32)
                for p in range(16):
                    bk = 4 + p % 2
                    pv = banks[bk][:, 0:384]

                    def qkvmm(e, p=p, pv=pv):
                        e.matmul(out=pv, lhsT=xln_fm[:, p * 128:(p + 1) * 128],
                                 rhs=wqkvbd[:, p * 384:(p + 1) * 384],
                                 start=True, stop=False)
                        return e.matmul(out=pv, lhsT=onesrowb,
                                        rhs=biasrow[0:1, p * 384:(p + 1) * 384],
                                        start=False, stop=True)
                    E("tensor", qkvmm,
                      reads=[xln_fmB, wqkvbdB, onesrowbB, biasrowB],
                      writes=[bankB[bk]])
                    src = pv.rearrange("p (s c d) -> p s c d", s=3, c=2)
                    dst = qkv3v[:, :, 2 * p:2 * p + 2, :]
                    E("vector", lambda e, src=src, dst=dst: e.tensor_copy(
                        out=dst, in_=src), reads=[bankB[bk]], writes=[qkv_tmB])
                def st_smult(dc):
                    pb = dc % 2
                    d0 = dc * DCH
                    qv = qkv_tm.rearrange("p (s c d) -> p s d c", s=3, c=32)[
                        :, 0, d0:d0 + DCH, :].unsqueeze(3).broadcast_to(
                        [128, DCH, 32, 32])
                    kv = qkv_tm.rearrange("p (s c d) -> p s d c", s=3, c=32)[
                        :, 1, d0:d0 + DCH, :].unsqueeze(2).broadcast_to(
                        [128, DCH, 32, 32])
                    s_out = S2[pb].rearrange("p (d c e) -> p d c e",
                                             d=DCH, c=32)
                    E("vector", lambda e, qv=qv, kv=kv, s_out=s_out:
                      e.tensor_tensor(out=s_out, in0=qv, in1=kv, op=ALU.mult),
                      reads=[qkv_tmB], writes=[S2B[pb]])

                def st_exp(dc):
                    pb = dc % 2
                    E("scalar", lambda e, s=S2[pb], p=P2[pb]: e.activation(
                        out=p, in_=s, func=AF.Exp, scale=0.125),
                      reads=[S2B[pb]], writes=[P2B[pb]])

                def st_esum(dc):
                    pb = dc % 2
                    d0 = dc * DCH
                    pview = P2[pb].rearrange("p (d c e) -> p d c e",
                                             d=DCH, c=32)
                    E("vector", lambda e, pview=pview, o=sums2[pb]:
                      e.reduce_sum(out=o.rearrange("p (d c) -> p d c", d=DCH),
                                   in_=pview, axis=AX.X),
                      reads=[P2B[pb]], writes=[sums2B[pb]])
                    E("vector", lambda e, i_=sums2[pb], o=rc2[pb]:
                      e.reciprocal(out=o, in_=i_),
                      reads=[sums2B[pb]], writes=[rc2B[pb]])
                    vsl = qkv_tm.rearrange("p (s c d) -> p s c d", s=3, c=32)[
                        :, 2, :, d0:d0 + DCH]
                    rcv = rc2[pb].rearrange("p (d c) -> p c d", d=DCH)
                    E("vector", lambda e, vsl=vsl, rcv=rcv,
                      o=v2[pb].rearrange("p (c d) -> p c d", c=32):
                      e.tensor_tensor(out=o, in0=vsl, in1=rcv, op=ALU.mult),
                      reads=[qkv_tmB, rc2B[pb]], writes=[v2B[pb]])

                def st_gmul(dc):
                    pb = dc % 2
                    vbc = v2[pb].rearrange("p (c d) -> p d c", c=32).unsqueeze(
                        3).broadcast_to([128, DCH, 32, 32])
                    E("gpsimd", lambda e, p=P2[pb], vbc=vbc,
                      o=tmp2[pb].rearrange("p (d c m) -> p d c m", d=DCH, c=32):
                      e.tensor_tensor(out=o, in0=p.rearrange(
                          "p (d c m) -> p d c m", d=DCH, c=32), in1=vbc,
                          op=ALU.mult),
                      reads=[P2B[pb], v2B[pb]], writes=[tmp2B[pb]])

                def st_csum(dc):
                    pb = dc % 2
                    d0 = dc * DCH
                    tview = tmp2[pb].rearrange("p (d c m) -> p d m c",
                                               d=DCH, c=32)
                    cview = ctx.rearrange("p (m d) -> p d m", m=32)[
                        :, d0:d0 + DCH, :]
                    E("vector", lambda e, tview=tview, cview=cview:
                      e.reduce_sum(out=cview, in_=tview, axis=AX.X),
                      reads=[tmp2B[pb]], writes=[ctxB])

                for i in range(NDC + 2):
                    if i < NDC:
                        st_smult(i)
                        st_exp(i)
                    if 1 <= i and i - 1 < NDC:
                        st_esum(i - 1)
                        st_gmul(i - 1)
                    if 2 <= i:
                        st_csum(i - 2)
                if debug and h == 0 and ti == 0:
                    E("sync", lambda e: e.dma_start(out=dbgQ, in_=qkv_tm),
                      reads=[qkv_tmB], dma="dDBG", extra=chain("dDBG"))
                    E("sync", lambda e: e.dma_start(out=dbgX, in_=ctx),
                      reads=[ctxB], dma="dDBG", extra=chain("dDBG"))
                for j in range(16):
                    bk = 2 + j % 2
                    pv = banks[bk][:, 128:256]
                    E("tensor", lambda e, j=j, pv=pv: e.transpose(
                        out=pv, in_=ctx[:, j * 128:(j + 1) * 128],
                        identity=eyef),
                      reads=[ctxB, eyefB], writes=[bankB[bk]])
                    cf = ctx_fm[j % 2]
                    E("vector", lambda e, pv=pv, cf=cf: e.tensor_copy(
                        out=cf, in_=pv), reads=[bankB[bk]],
                      writes=[ctx_fmB[j % 2]])
                    bo = 6 + j % 2
                    po = banks[bo][:, 0:128]
                    E("tensor", lambda e, cf=cf, po=po: e.matmul(
                        out=po, lhsT=woutbd, rhs=cf, start=True, stop=True),
                      reads=[ctx_fmB[j % 2], woutbdB], writes=[bankB[bo]])
                    tb = t3[j % 2]
                    E("scalar", lambda e, po=po, tb=tb: e.activation(
                        out=tb, in_=po, func=AF.Identity, bias=bout2,
                        scale=1.0),
                      reads=[bankB[bo], bout2B], writes=[t3B[j % 2]])
                    asl = A[:, j * T_PASS + ti * 128: j * T_PASS + ti * 128 + 128]
                    E("vector", lambda e, asl=asl, tb=tb: e.tensor_add(
                        out=asl, in0=asl, in1=tb),
                      reads=[t3B[j % 2], Ab[j]], writes=[Ab[j]])
                if prelude_units:
                    ntake = -(-len(prelude_units) // (NT - ti))
                    for u in prelude_units[:ntake]:
                        pre_unit(*u)
                    del prelude_units[:ntake]

            if debug:
                E("sync", lambda e, h=h: e.dma_start(out=dbgA[h], in_=A),
                  reads=list(Ab), dma="dDBG", extra=chain("dDBG"))
            arena.reset()
            ln2x, _ln2 = arena.alloc("ln2x", 128, HC * T_PASS, BF16)
            ln2xB = [Buf(f"ln2x{k}") for k in range(HC)]
            abuf, _ab = arena.alloc("abuf", 128, ICE * T_PASS, BF16)
            abufB = [Buf(f"abuf{j}") for j in range(ICE)]
            wgb, wgbB = zip(*[arena.alloc(f"wgb{i}", 128, 2048, BF16)
                              for i in range(2)])
            wub, wubB = zip(*[arena.alloc(f"wub{i}", 128, 2048, BF16)
                              for i in range(2)])
            wdb, wdbB = zip(*[arena.alloc(f"wdb{i}", 128, 1408, BF16)
                              for i in range(2)])
            s_t, s_tB = zip(*[arena.alloc(f"s_t{i}", 128, 512, F32)
                              for i in range(2)])
            t5, t5B = zip(*[arena.alloc(f"t5_{i}", 128, 512, BF16)
                            for i in range(2)])
            C_b, C_bB = arena.alloc("C_b", 128, 1024, F32)
            R_b, R_bB = arena.alloc("R_b", 128, 512, F32)
            MR_b, MR_bB = arena.alloc("MR_b", 128, 512, F32)
            rows = {}
            for nm in ("mrow", "msqrow", "ra", "rb"):
                rows[nm] = arena.alloc("r_" + nm, 1, 512, F32)
            comb_all, comb_allB = arena.alloc("comb_all", 128, 128, F32)
            comb_fm, comb_fmB = arena.alloc("comb_fm", 16, 1024, BF16)
            tk = {}
            for nm in ("sorig", "swork", "msk", "acc4"):
                tk[nm] = arena.alloc("tk_" + nm, 128, 16, F32)
            for nm in ("mx", "nmx", "z4", "rz"):
                tk[nm] = arena.alloc("tk_" + nm, 128, 1, F32)

            for blk in range(NB):
                c0 = blk * 512

                def statmms(e, c0=c0):
                    last = None
                    for k in range(HC):
                        last = e.matmul(
                            out=banks[0][0:1, :], lhsT=onescol,
                            rhs=A[:, k * T_PASS + c0: k * T_PASS + c0 + 512],
                            start=(k == 0), stop=(k == HC - 1))
                    return last
                E("tensor", statmms, reads=list(Ab) + [onescolB],
                  writes=[bankB[0]])
                for k in range(HC):
                    sq = s_t[k % 2]
                    E("scalar", lambda e, k=k, sq=sq, c0=c0: e.activation(
                        out=sq, in_=A[:, k * T_PASS + c0: k * T_PASS + c0 + 512],
                        func=AF.Square),
                      reads=[Ab[k]], writes=[s_tB[k % 2]])
                    E("tensor", lambda e, sq=sq, k=k: e.matmul(
                        out=banks[1][0:1, :], lhsT=onescol, rhs=sq,
                        start=(k == 0), stop=(k == HC - 1)),
                      reads=[s_tB[k % 2], onescolB], writes=[bankB[1]])
                E("scalar", lambda e: e.activation(
                    out=rows["mrow"][0], in_=banks[0][0:1, :], func=AF.Identity,
                    bias=0.0, scale=1.0 / 2048),
                  reads=[bankB[0]], writes=[rows["mrow"][1]])
                E("scalar", lambda e: e.activation(
                    out=rows["msqrow"][0], in_=banks[1][0:1, :],
                    func=AF.Identity, bias=0.0, scale=1.0 / 2048),
                  reads=[bankB[1]], writes=[rows["msqrow"][1]])
                E("vector", lambda e: e.tensor_mul(
                    out=rows["ra"][0], in0=rows["mrow"][0], in1=rows["mrow"][0]),
                  reads=[rows["mrow"][1]], writes=[rows["ra"][1]])
                E("vector", lambda e: e.tensor_sub(
                    out=rows["rb"][0], in0=rows["msqrow"][0], in1=rows["ra"][0]),
                  reads=[rows["msqrow"][1], rows["ra"][1]],
                  writes=[rows["rb"][1]])
                E("scalar", lambda e: e.activation(
                    out=rows["ra"][0], in_=rows["rb"][0], func=AF.Sqrt,
                    bias=epsv[0:1, :], scale=1.0),
                  reads=[rows["rb"][1], epsB], writes=[rows["ra"][1]])
                E("vector", lambda e: e.reciprocal(
                    out=rows["rb"][0], in_=rows["ra"][0]),
                  reads=[rows["ra"][1]], writes=[rows["rb"][1]])
                E("vector", lambda e: e.tensor_mul(
                    out=rows["ra"][0], in0=rows["mrow"][0], in1=rows["rb"][0]),
                  reads=[rows["mrow"][1], rows["rb"][1]],
                  writes=[rows["ra"][1]])
                E("tensor", lambda e: e.matmul(
                    out=banks[2][:, :], lhsT=onesrowf, rhs=rows["rb"][0],
                    start=True, stop=True),
                  reads=[rows["rb"][1], onesrowfB], writes=[bankB[2]])
                E("tensor", lambda e: e.matmul(
                    out=banks[3][:, :], lhsT=onesrowf, rhs=rows["ra"][0],
                    start=True, stop=True),
                  reads=[rows["ra"][1], onesrowfB], writes=[bankB[3]])
                E("vector", lambda e: e.tensor_copy(out=R_b, in_=banks[2][:, :]),
                  reads=[bankB[2]], writes=[R_bB])
                E("vector", lambda e: e.tensor_copy(out=MR_b, in_=banks[3][:, :]),
                  reads=[bankB[3]], writes=[MR_bB])
                for k in range(HC):
                    sq = s_t[k % 2]
                    asl = A[:, k * T_PASS + c0: k * T_PASS + c0 + 512]
                    E("vector", lambda e, asl=asl, sq=sq: e.tensor_mul(
                        out=sq, in0=asl, in1=R_b),
                      reads=[Ab[k], R_bB], writes=[s_tB[k % 2]])
                    lsl = ln2x[:, k * T_PASS + c0: k * T_PASS + c0 + 512]
                    E("vector", lambda e, lsl=lsl, sq=sq: e.tensor_sub(
                        out=lsl, in0=sq, in1=MR_b),
                      reads=[s_tB[k % 2], MR_bB], writes=[ln2xB[k]])

            for ts in range(NT):
                def gatemms(e, ts=ts):
                    last = None
                    for k in range(HC):
                        last = e.matmul(
                            out=banks[4][:, 0:16],
                            lhsT=ln2x[:, k * T_PASS + ts * 128:
                                      k * T_PASS + ts * 128 + 128],
                            rhs=gtw[:, k * 16:(k + 1) * 16],
                            start=(k == 0), stop=(k == HC - 1))
                    return last
                E("tensor", gatemms, reads=list(ln2xB) + [gtwB],
                  writes=[bankB[4]])
                lg = banks[4][:, 0:16]
                E("vector", lambda e, lg=lg: e.reduce_max(
                    out=tk["mx"][0], in_=lg, axis=AX.X),
                  reads=[bankB[4]], writes=[tk["mx"][1]])
                E("vector", lambda e: e.tensor_scalar_mul(
                    out=tk["nmx"][0], in0=tk["mx"][0], scalar1=-1.0),
                  reads=[tk["mx"][1]], writes=[tk["nmx"][1]])
                E("scalar", lambda e, lg=lg: e.activation(
                    out=tk["sorig"][0], in_=lg, func=AF.Exp,
                    bias=tk["nmx"][0], scale=1.0),
                  reads=[bankB[4], tk["nmx"][1]], writes=[tk["sorig"][1]])
                E("vector", lambda e: e.tensor_copy(
                    out=tk["swork"][0], in_=tk["sorig"][0]),
                  reads=[tk["sorig"][1]], writes=[tk["swork"][1]])
                E("vector", lambda e: e.memset(tk["acc4"][0], 0.0),
                  writes=[tk["acc4"][1]])
                for it in range(4):
                    E("vector", lambda e: e.reduce_max(
                        out=tk["mx"][0], in_=tk["swork"][0], axis=AX.X),
                      reads=[tk["swork"][1]], writes=[tk["mx"][1]])
                    E("vector", lambda e: e.tensor_scalar(
                        out=tk["msk"][0], in0=tk["swork"][0],
                        scalar1=tk["mx"][0], scalar2=None, op0=ALU.is_equal),
                      reads=[tk["swork"][1], tk["mx"][1]], writes=[tk["msk"][1]])
                    E("vector", lambda e: e.scalar_tensor_tensor(
                        out=tk["swork"][0], in0=tk["msk"][0], scalar=-1e30,
                        in1=tk["swork"][0], op0=ALU.mult, op1=ALU.add),
                      reads=[tk["msk"][1], tk["swork"][1]],
                      writes=[tk["swork"][1]])
                    E("vector", lambda e: e.tensor_add(
                        out=tk["acc4"][0], in0=tk["acc4"][0], in1=tk["msk"][0]),
                      reads=[tk["msk"][1], tk["acc4"][1]],
                      writes=[tk["acc4"][1]])
                E("vector", lambda e: e.tensor_mul(
                    out=tk["msk"][0], in0=tk["acc4"][0], in1=tk["sorig"][0]),
                  reads=[tk["acc4"][1], tk["sorig"][1]], writes=[tk["msk"][1]])
                E("vector", lambda e: e.reduce_sum(
                    out=tk["z4"][0], in_=tk["msk"][0], axis=AX.X),
                  reads=[tk["msk"][1]], writes=[tk["z4"][1]])
                E("vector", lambda e: e.reciprocal(
                    out=tk["rz"][0], in_=tk["z4"][0]),
                  reads=[tk["z4"][1]], writes=[tk["rz"][1]])
                E("vector", lambda e, ts=ts: e.tensor_scalar_mul(
                    out=comb_all[:, ts * 16:(ts + 1) * 16], in0=tk["msk"][0],
                    scalar1=tk["rz"][0]),
                  reads=[tk["msk"][1], tk["rz"][1]], writes=[comb_allB])
            for ts in range(NT):
                pv = banks[5][0:16, 0:128]
                E("tensor", lambda e, ts=ts, pv=pv: e.transpose(
                    out=pv, in_=comb_all[:, ts * 16:(ts + 1) * 16],
                    identity=eyef),
                  reads=[comb_allB, eyefB], writes=[bankB[5]])
                E("vector", lambda e, ts=ts, pv=pv: e.tensor_copy(
                    out=comb_fm[:, ts * 128:(ts + 1) * 128], in_=pv),
                  reads=[bankB[5]], writes=[comb_fmB])

            if debug:
                E("sync", lambda e, h=h, l=ln2x: e.dma_start(out=dbgL[h], in_=l),
                  reads=list(ln2xB), dma="dDBG", extra=chain("dDBG"))
                E("sync", lambda e, h=h, c=comb_all: e.dma_start(out=dbgC[h], in_=c),
                  reads=[comb_allB], dma="dDBG", extra=chain("dDBG"))
            for ex in range(NEXP + 2):
                shared = ex >= NEXP
                ioff = 0 if ex != NEXP + 1 else 11
                joff = 0 if ex != NEXP + 1 else 1408
                if not shared:
                    for blk in range(NB):
                        E("tensor", lambda e, ex=ex, blk=blk: e.matmul(
                            out=banks[6][:, :],
                            lhsT=sel[:, ex * 128:(ex + 1) * 128],
                            rhs=comb_fm[:, blk * 512:(blk + 1) * 512],
                            start=True, stop=True),
                          reads=[selB, comb_fmB], writes=[bankB[6]])
                        E("vector", lambda e, blk=blk: e.tensor_copy(
                            out=C_b[:, blk * 512:(blk + 1) * 512],
                            in_=banks[6][:, :]),
                          reads=[bankB[6]], writes=[C_bB])
                for isl in range(ICE):
                    par = isl % 2
                    E("sync", lambda e, s=wgsc_d[ex][isl], d=wgb[par]:
                      e.dma_start(out=d, in_=s),
                      reads=[wgscB[ex]], writes=[wgbB[par]], dma=f"dG{par}")
                    E("scalar", lambda e, s=wusc_d[ex][isl], d=wub[par]:
                      e.dma_start(out=d, in_=s),
                      reads=[wuscB[ex]], writes=[wubB[par]], dma=f"dU{par}")
                    gb0, gb1 = 4 * par, 4 * par + 1
                    ub0, ub1 = 4 * par + 2, 4 * par + 3

                    def gumms(e, par=par):
                        last = None
                        for k in range(HC):
                            st_, sp = (k == 0), (k == HC - 1)
                            for blk in range(NB):
                                gbk = banks[4 * par + blk]
                                ubk = banks[4 * par + 2 + blk]
                                e.matmul(
                                    out=gbk[:, :],
                                    lhsT=wgb[par][:, k * 128:(k + 1) * 128],
                                    rhs=ln2x[:, k * T_PASS + blk * 512:
                                             k * T_PASS + blk * 512 + 512],
                                    start=st_, stop=sp)
                                last = e.matmul(
                                    out=ubk[:, :],
                                    lhsT=wub[par][:, k * 128:(k + 1) * 128],
                                    rhs=ln2x[:, k * T_PASS + blk * 512:
                                             k * T_PASS + blk * 512 + 512],
                                    start=st_, stop=sp)
                        return last
                    E("tensor", gumms,
                      reads=[wgbB[par], wubB[par]] + list(ln2xB),
                      writes=[bankB[gb0], bankB[gb1], bankB[ub0], bankB[ub1]])
                    for blk in range(NB):
                        gbk = banks[4 * par + blk]
                        ubk = banks[4 * par + 2 + blk]
                        gB, uB = bankB[4 * par + blk], bankB[4 * par + 2 + blk]
                        stv = s_t[blk]
                        E("scalar", lambda e, gbk=gbk, stv=stv: e.activation(
                            out=stv, in_=gbk[:, :], func=AF.Silu),
                          reads=[gB], writes=[s_tB[blk]])
                        aslice = abuf[:, isl * T_PASS + blk * 512:
                                      isl * T_PASS + blk * 512 + 512]
                        if shared:
                            E("vector", lambda e, stv=stv, ubk=ubk,
                              aslice=aslice: e.tensor_mul(
                                  out=aslice, in0=stv, in1=ubk[:, :]),
                              reads=[s_tB[blk], uB], writes=[abufB[isl]])
                        else:
                            tv = t5[blk]
                            E("vector", lambda e, stv=stv, ubk=ubk, tv=tv:
                              e.tensor_mul(out=tv, in0=stv, in1=ubk[:, :]),
                              reads=[s_tB[blk], uB], writes=[t5B[blk]])
                            E("gpsimd", lambda e, tv=tv, aslice=aslice,
                              blk=blk: e.tensor_tensor(
                                  out=aslice, in0=tv,
                                  in1=C_b[:, blk * 512:(blk + 1) * 512],
                                  op=ALU.mult),
                              reads=[t5B[blk], C_bB], writes=[abufB[isl]])
                for hc in range(HC):
                    par = hc % 2
                    E("sync", lambda e, s=wdsc_d[ex][hc], d=wdb[par]:
                      e.dma_start(out=d, in_=s),
                      reads=[wdscB[ex]], writes=[wdbB[par]], dma=f"dD{par}")
                    yb0, yb1 = 4 + 2 * par, 5 + 2 * par

                    def dmms(e, par=par):
                        last = None
                        for j in range(ICE):
                            st_, sp = (j == 0), (j == ICE - 1)
                            for blk in range(NB):
                                ybk = banks[4 + 2 * par + blk]
                                last = e.matmul(
                                    out=ybk[:, :],
                                    lhsT=wdb[par][:, j * 128:(j + 1) * 128],
                                    rhs=abuf[:, j * T_PASS + blk * 512:
                                             j * T_PASS + blk * 512 + 512],
                                    start=st_, stop=sp)
                        return last
                    E("tensor", dmms, reads=[wdbB[par]] + list(abufB),
                      writes=[bankB[yb0], bankB[yb1]])
                    for blk in range(NB):
                        ybk = banks[4 + 2 * par + blk]
                        yB = bankB[4 + 2 * par + blk]
                        asl = A[:, hc * T_PASS + blk * 512:
                                hc * T_PASS + blk * 512 + 512]
                        E("vector", lambda e, asl=asl, ybk=ybk: e.tensor_add(
                            out=asl, in0=asl, in1=ybk[:, :]),
                          reads=[yB, Ab[hc]], writes=[Ab[hc]])

            arena.reset()
            out_tm, out_tmB = zip(*[arena.alloc(f"out_tm{i}", 128, 2048, F32)
                                    for i in range(2)])
            for ti in range(NT):
                ob = ti % 2
                for k in range(HC):
                    bk = k % 2
                    pv = banks[bk][:, 0:128]
                    E("tensor", lambda e, k=k, ti=ti, pv=pv: e.transpose(
                        out=pv,
                        in_=A[:, k * T_PASS + ti * 128: k * T_PASS + ti * 128 + 128],
                        identity=eyef),
                      reads=[Ab[k], eyefB], writes=[bankB[bk]])
                    E("vector", lambda e, k=k, pv=pv, d=out_tm[ob]:
                      e.tensor_copy(out=d[:, k * 128:(k + 1) * 128], in_=pv),
                      reads=[bankB[bk]], writes=[out_tmB[ob]])
                E("sync", lambda e, d=out_tm[ob], r0=tok0 + ti * 128:
                  e.dma_start(out=out_d[r0:r0 + 128, :], in_=d),
                  reads=[out_tmB[ob]], dma=f"dO{ob}")

        plan.wait_only("sync", [(s, plan.cnt.get(s, 0)) for s in sem_names
                                if s.startswith("d")])

        def run_ops(handle, eng):
            for fn, waits, sem, amt in plan.ops[eng]:
                for s, v in waits:
                    handle.wait_ge(sems[s], v)
                if fn is not None:
                    ins = fn(handle)
                    if sem is not None:
                        ins.then_inc(sems[sem], amt)

        @block.sync
        def _(e):
            run_ops(e, "sync")

        @block.vector
        def _(e):
            run_ops(e, "vector")

        @block.scalar
        def _(e):
            run_ops(e, "scalar")

        @block.tensor
        def _(e):
            run_ops(e, "tensor")

        @block.gpsimd
        def _(e):
            run_ops(e, "gpsimd")

    return nc


def host_prepare(inputs):
    Wqkv = np.asarray(inputs["L0_attn_Wqkv"], dtype=np.float32)
    bqkv = np.asarray(inputs["L0_attn_bqkv"], dtype=np.float32)
    Wout = np.asarray(inputs["L0_attn_Wout"], dtype=np.float32)
    bout = np.asarray(inputs["L0_attn_bout"], dtype=np.float32)
    gate_w = np.asarray(inputs["L0_ffn_gate_w"], dtype=np.float32)

    wbd = np.zeros((16, 2, 64, 3, 2, 64), dtype=np.float32)
    wq5 = Wqkv.reshape(16, 2, 64, 3, 64)
    for c2 in range(2):
        wbd[:, c2, :, :, c2, :] = wq5[:, c2]
    wqkv_bd = wbd.reshape(16, 128, 384)

    bias_pairs = bqkv.reshape(16, 2, 3, 64).transpose(0, 2, 1, 3).reshape(1, 6144)

    wout_bd = np.zeros((128, 128), dtype=np.float32)
    wout_bd[0:64, 0:64] = Wout
    wout_bd[64:128, 64:128] = Wout

    bout2 = np.concatenate([bout, bout]).reshape(128, 1)
    gate_wT = np.ascontiguousarray(gate_w.T)
    eye128 = np.eye(128, dtype=np.float32)

    return {
        "w_gate": np.ascontiguousarray(inputs["L0_ffn_w_gate"], dtype=np.float32),
        "w_up": np.ascontiguousarray(inputs["L0_ffn_w_up"], dtype=np.float32),
        "w_down": np.ascontiguousarray(inputs["L0_ffn_w_down"], dtype=np.float32),
        "sw_gate": np.ascontiguousarray(inputs["L0_ffn_sw_gate"], dtype=np.float32),
        "sw_up": np.ascontiguousarray(inputs["L0_ffn_sw_up"], dtype=np.float32),
        "sw_down": np.ascontiguousarray(inputs["L0_ffn_sw_down"], dtype=np.float32),
        "wqkv_bd": wqkv_bd,
        "bias_pairs": bias_pairs,
        "wout_bd": wout_bd,
        "bout2": bout2,
        "gate_wT": gate_wT,
        "eye128": eye128,
    }


def kernel(**inputs):
    x = np.asarray(inputs["L0_attn_x"], dtype=np.float32).reshape(16384, 2048)
    shared = host_prepare(inputs)
    nc = build_program()
    in_maps = []
    for c in range(N_CORES):
        m = dict(shared)
        m["xin"] = np.ascontiguousarray(x[c * T_CORE:(c + 1) * T_CORE, :])
        in_maps.append(m)
    res = run_bass_kernel_spmd(nc, in_maps, core_ids=list(range(N_CORES)))
    out = np.concatenate([r["out"] for r in res.results], axis=0)
    return out.reshape(1, 16384, 2048)


if __name__ == "__main__":
    nc = build_program()
    print("build ok")


# version 10
# speedup vs baseline: 1.0357x; 1.0315x over previous
import numpy as np
import concourse.bass as bass
import concourse.mybir as mybir
from concourse.bass_utils import run_bass_kernel_spmd

F32 = mybir.dt.float32
BF16 = mybir.dt.bfloat16
U8 = mybir.dt.uint8
AF = mybir.ActivationFunctionType
ALU = mybir.AluOpType
AX = mybir.AxisListType

N_CORES = 8
T_CORE = 2048
N_PASS = 2
T_PASS = 1024
NT = T_PASS // 128
NB = T_PASS // 512
HC = 16
ICE = 11
NEXP = 16
EPS = 1e-5
DCH = 1
NDC = 64 // DCH

ENG = ("sync", "vector", "scalar", "tensor", "gpsimd")
ESEM = {"vector": "v", "scalar": "s", "tensor": "t", "gpsimd": "g"}


class Buf:
    __slots__ = ("name", "w", "readers")

    def __init__(self, name):
        self.name = name
        self.w = None
        self.readers = []


class Plan:
    def __init__(self):
        self.ops = {e: [] for e in ENG}
        self.cnt = {}
        self.known = {e: {} for e in ENG}

    def emit(self, eng, fn, reads=(), writes=(), dma=None, extra=()):
        waits = {}

        def need(sw):
            if sw is None:
                return
            s, v = sw
            if self.known[eng].get(s, 0) >= v:
                return
            if waits.get(s, 0) < v:
                waits[s] = v

        for b in reads:
            need(b.w)
        for b in writes:
            need(b.w)
            for sw in b.readers:
                need(sw)
        for sw in extra:
            need(sw)
        sem = dma if dma else ESEM[eng]
        amt = 16 if dma else 1
        self.cnt[sem] = self.cnt.get(sem, 0) + amt
        v = self.cnt[sem]
        for s, w in waits.items():
            self.known[eng][s] = max(self.known[eng].get(s, 0), w)
        self.ops[eng].append((fn, sorted(waits.items()), sem, amt))
        for b in writes:
            b.w = (sem, v)
            b.readers = []
        for b in reads:
            b.readers.append((sem, v))
        return (sem, v)

    def wait_only(self, eng, pairs):
        waits = {}
        for s, v in pairs:
            if self.known[eng].get(s, 0) >= v:
                continue
            waits[s] = max(waits.get(s, 0), v)
        if not waits:
            return
        for s, w in waits.items():
            self.known[eng][s] = max(self.known[eng].get(s, 0), w)
        self.ops[eng].append((None, sorted(waits.items()), None, 0))


class Arena:
    SIZES = {F32: 4, BF16: 2, U8: 1}

    def __init__(self, ap, nbytes):
        self.base = ap
        self.nbytes = nbytes
        self.off = 0
        self.ghost = {}
        self.live = []
        self.peak = 0

    def begin(self, tag):
        self.reset()
        self._tag = tag
        if not hasattr(self, "_layouts"):
            self._layouts = {}
        self._rec = []

    def reset(self):
        if getattr(self, "_tag", None) is not None:
            prev = self._layouts.setdefault(self._tag, self._rec)
            assert prev == self._rec, (
                f"phase '{self._tag}' layout differs across passes:\n"
                f"{prev}\nvs\n{self._rec}")
            self._tag = None
        deps = dict(self.ghost)
        for b in self.live:
            sws = ([b.w] if b.w else []) + b.readers
            for s, v in sws:
                deps[s] = max(deps.get(s, 0), v)
        self.ghost = deps
        self.live = []
        self.peak = max(self.peak, self.off)
        self.off = 0

    def alloc(self, name, parts, n, dtype):
        nbytes = n * self.SIZES[dtype]
        off = (self.off + 3) & ~3
        assert off + nbytes <= self.nbytes, (
            f"arena overflow at {name}: {off + nbytes} > {self.nbytes}")
        self.off = off + nbytes
        if getattr(self, "_tag", None) is not None:
            self._rec.append((name, off, nbytes))
        view = self.base[0:parts, off:off + nbytes].bitcast(dtype)
        b = Buf(name)
        if self.ghost:
            b.readers = list(self.ghost.items())
        self.live.append(b)
        return view, b


def build_program(debug=False):
    nc = bass.Bass("TRN2", target_bir_lowering=False)

    xin = nc.dram_tensor("xin", [T_CORE, 2048], F32, kind="ExternalInput").ap()
    wg_d = nc.dram_tensor("w_gate", [16, 2048, 1408], F32, kind="ExternalInput").ap()
    wu_d = nc.dram_tensor("w_up", [16, 2048, 1408], F32, kind="ExternalInput").ap()
    wd_d = nc.dram_tensor("w_down", [16, 1408, 2048], F32, kind="ExternalInput").ap()
    swg_d = nc.dram_tensor("sw_gate", [2048, 2816], F32, kind="ExternalInput").ap()
    swu_d = nc.dram_tensor("sw_up", [2048, 2816], F32, kind="ExternalInput").ap()
    swd_d = nc.dram_tensor("sw_down", [2816, 2048], F32, kind="ExternalInput").ap()
    wqkvbd_d = nc.dram_tensor("wqkv_bd", [16, 128, 384], F32, kind="ExternalInput").ap()
    biasp_d = nc.dram_tensor("bias_pairs", [1, 6144], F32, kind="ExternalInput").ap()
    woutbd_d = nc.dram_tensor("wout_bd", [128, 128], F32, kind="ExternalInput").ap()
    bout2_d = nc.dram_tensor("bout2", [128, 1], F32, kind="ExternalInput").ap()
    gwt_d = nc.dram_tensor("gate_wT", [2048, 16], F32, kind="ExternalInput").ap()
    eye_d = nc.dram_tensor("eye128", [128, 128], F32, kind="ExternalInput").ap()
    out_d = nc.dram_tensor("out", [T_CORE, 2048], F32, kind="ExternalOutput").ap()
    wgsc_d = nc.dram_tensor("wgsc", [18, 11, 128, 2048], BF16, kind="Internal").ap()
    wusc_d = nc.dram_tensor("wusc", [18, 11, 128, 2048], BF16, kind="Internal").ap()
    wdsc_d = nc.dram_tensor("wdsc", [18, 16, 128, 1408], BF16, kind="Internal").ap()
    if debug:
        dbgA = [nc.dram_tensor(f"dbgA{h}", [128, 16384], F32,
                               kind="ExternalOutput").ap() for h in range(2)]
        dbgL = [nc.dram_tensor(f"dbgL{h}", [128, 16384], BF16,
                               kind="ExternalOutput").ap() for h in range(2)]
        dbgC = [nc.dram_tensor(f"dbgC{h}", [128, 128], F32,
                               kind="ExternalOutput").ap() for h in range(2)]
        dbgQ = nc.dram_tensor("dbgQ", [128, 6144], BF16,
                              kind="ExternalOutput").ap()
        dbgX = nc.dram_tensor("dbgX", [128, 2048], F32,
                              kind="ExternalOutput").ap()

    plan = Plan()
    from contextlib import ExitStack
    with ExitStack() as st:
        def sbt(name, shape, dtype):
            return st.enter_context(nc.sbuf_tensor(name, shape, dtype))

        A_t = sbt("A", [128, HC * T_PASS], F32)
        wqkvbd_t = sbt("wqkvbd", [128, 6144], BF16)
        woutbd_t = sbt("woutbd", [128, 128], BF16)
        bout2_t = sbt("bout2_sb", [128, 1], F32)
        gtw_t = sbt("gtw", [128, 256], BF16)
        sel_t = sbt("sel", [16, 2048], BF16)
        eyef_t = sbt("eyef", [128, 128], F32)
        eyeb_t = sbt("eyeb", [128, 128], BF16)
        onescol_t = sbt("onescol", [128, 1], F32)
        onesrowf_t = sbt("onesrowf", [1, 128], F32)
        onesrowb_t = sbt("onesrowb", [1, 128], BF16)
        eps_t = sbt("eps_sb", [128, 1], F32)
        ARENA_BYTES = 116256
        arena_t = sbt("arena", [128, ARENA_BYTES], U8)

        psum_t = [st.enter_context(nc.psum_tensor(f"bank{i}", [128, 512], F32))
                  for i in range(8)]
        banks = [p.ap() for p in psum_t]
        bankB = [Buf(f"bank{i}") for i in range(8)]

        sem_names = ["v", "s", "t", "g", "dC", "dDBG", "dX0", "dX1",
                     "dG0", "dG1", "dU0", "dU1", "dD0", "dD1",
                     "dPF0", "dPF1", "dPW0", "dPW1", "dO0", "dO1"]
        sems = {n: st.enter_context(nc.semaphore("sem_" + n)) for n in sem_names}
        block = st.enter_context(nc.Block())

        arena = Arena(arena_t.ap(), ARENA_BYTES)

        A = A_t.ap()
        Ab = [Buf(f"A{k}") for k in range(HC)]
        wgscB = [Buf(f"wgsc{e}") for e in range(NEXP + 2)]
        wuscB = [Buf(f"wusc{e}") for e in range(NEXP + 2)]
        wdscB = [Buf(f"wdsc{e}") for e in range(NEXP + 2)]
        wqkvbd = wqkvbd_t.ap(); wqkvbdB = Buf("wqkvbd")
        woutbd = woutbd_t.ap(); woutbdB = Buf("woutbd")
        bout2 = bout2_t.ap(); bout2B = Buf("bout2")
        gtw = gtw_t.ap(); gtwB = Buf("gtw")
        sel = sel_t.ap(); selB = Buf("sel")
        eyef = eyef_t.ap(); eyefB = Buf("eyef")
        eyeb = eyeb_t.ap(); eyebB = Buf("eyeb")
        onescol = onescol_t.ap(); onescolB = Buf("onescol")
        onesrowf = onesrowf_t.ap(); onesrowfB = Buf("onesrowf")
        onesrowb = onesrowb_t.ap(); onesrowbB = Buf("onesrowb")
        epsv = eps_t.ap(); epsB = Buf("eps")

        E = plan.emit

        def chain(sem):
            return [(sem, plan.cnt.get(sem, 0))]

        stg, stgB = arena.alloc("stg", 128, 6144, F32)
        E("sync", lambda e, d=stg: e.dma_start(out=d, in_=wqkvbd_d.rearrange(
            "k p i -> p k i")), writes=[stgB], dma="dC", extra=chain("dC"))
        E("vector", lambda e, s=stg: e.tensor_copy(out=wqkvbd, in_=s),
          reads=[stgB], writes=[wqkvbdB])
        stg3, stg3B = arena.alloc("stg3", 128, 128, F32)
        E("sync", lambda e, d=stg3: e.dma_start(out=d, in_=woutbd_d),
          writes=[stg3B], dma="dC", extra=chain("dC"))
        E("vector", lambda e, s=stg3: e.tensor_copy(out=woutbd, in_=s),
          reads=[stg3B], writes=[woutbdB])
        E("sync", lambda e: e.dma_start(out=bout2, in_=bout2_d),
          writes=[bout2B], dma="dC", extra=chain("dC"))
        stg4, stg4B = arena.alloc("stg4", 128, 256, F32)
        E("sync", lambda e, d=stg4: e.dma_start(out=d, in_=gwt_d.rearrange(
            "(k p) e -> p k e", p=128)), writes=[stg4B], dma="dC", extra=chain("dC"))
        E("vector", lambda e, s=stg4: e.tensor_copy(out=gtw, in_=s),
          reads=[stg4B], writes=[gtwB])
        E("sync", lambda e: e.dma_start(out=eyef, in_=eye_d),
          writes=[eyefB], dma="dC", extra=chain("dC"))
        E("vector", lambda e: e.tensor_copy(out=eyeb, in_=eyef),
          reads=[eyefB], writes=[eyebB])
        E("vector", lambda e: e.memset(onescol, 1.0), writes=[onescolB])
        E("vector", lambda e: e.memset(epsv, EPS), writes=[epsB])
        E("vector", lambda e: e.memset(onesrowf, 1.0), writes=[onesrowfB])
        E("vector", lambda e: e.memset(onesrowb, 1.0), writes=[onesrowbB])
        ioe, ioeB = arena.alloc("ioe", 16, 2048, F32)
        iop, iopB = arena.alloc("iop", 16, 2048, F32)
        E("gpsimd", lambda e, d=ioe: e.iota(out=d, pattern=[[1, 16], [0, 128]],
          base=0, channel_multiplier=0, allow_small_or_imprecise_dtypes=True),
          writes=[ioeB])
        E("gpsimd", lambda e, d=iop: e.iota(out=d, pattern=[[0, 16], [0, 128]],
          base=0, channel_multiplier=1, allow_small_or_imprecise_dtypes=True),
          writes=[iopB])
        E("vector", lambda e: e.tensor_tensor(out=sel, in0=ioe, in1=iop,
          op=ALU.is_equal), reads=[ioeB, iopB], writes=[selB])

        qkv3 = None

        for h in range(N_PASS):
            tok0 = h * T_PASS
            arena.begin("attn")
            biasrow, biasrowB = arena.alloc("biasrow", 1, 6144, BF16)
            bstg, bstgB = arena.alloc("bstg", 1, 512, F32)
            for bc in range(12):
                E("sync", lambda e, d=bstg, bc=bc: e.dma_start(
                    out=d, in_=biasp_d[:, bc * 512:(bc + 1) * 512]),
                  writes=[bstgB], dma="dC", extra=chain("dC"))
                E("vector", lambda e, s=bstg, bc=bc: e.tensor_copy(
                    out=biasrow[:, bc * 512:(bc + 1) * 512], in_=s),
                  reads=[bstgB], writes=[biasrowB])
            x_tm, x_tmB = zip(*[arena.alloc(f"x_tm{i}", 128, 2048, F32)
                                for i in range(2)])
            qkv_tm, qkv_tmB = arena.alloc("qkv_tm", 128, 6144, BF16)
            xln_tm, xln_tmB = arena.alloc("xln_tm", 128, 2048, BF16)
            xln_fm, xln_fmB = arena.alloc("xln_fm", 128, 2048, BF16)
            S2, S2B = zip(*[arena.alloc(f"S2_{i}", 128, 1024 * DCH, BF16)
                            for i in range(2)])
            P2, P2B = zip(*[arena.alloc(f"P2_{i}", 128, 1024 * DCH, BF16)
                            for i in range(2)])
            tmp2, tmp2B = zip(*[arena.alloc(f"tmp2_{i}", 128, 1024 * DCH, BF16)
                                for i in range(2)])
            sums2, sums2B = zip(*[arena.alloc(f"sums2_{i}", 128, 32 * DCH, F32)
                                  for i in range(2)])
            rc2, rc2B = zip(*[arena.alloc(f"rc2_{i}", 128, 32 * DCH, F32)
                              for i in range(2)])
            v2, v2B = zip(*[arena.alloc(f"v2_{i}", 128, 32 * DCH, BF16)
                            for i in range(2)])
            ctx, ctxB = arena.alloc("ctx", 128, 2048, F32)
            xsq, xsqB = ctx, ctxB
            ctx_fm, ctx_fmB = zip(*[arena.alloc(f"ctxfm{i}", 128, 128, BF16)
                                    for i in range(2)])
            t3, t3B = zip(*[arena.alloc(f"t3_{i}", 128, 128, F32)
                            for i in range(1)]); t3 = t3 * 2; t3B = t3B * 2
            sc = {}
            for nm in ("rs", "rq", "mean", "msq", "m2", "var", "std", "rstd",
                       "mrneg"):
                sc[nm] = arena.alloc("ln1_" + nm, 128, 1, F32)

            prelude_units = []
            pf, pfB = zip(*[arena.alloc(f"pf{i}", 128, 2048, F32)
                            for i in range(2)])
            pbv, pbB = zip(*[arena.alloc(f"pb{i}", 128, 2048, BF16)
                             for i in range(2)])
            if h == 0:
                pstate = {"n": 0, "pf": pf, "pfB": pfB, "pb": pbv,
                          "pbB": pbB, "np": 2}

                def pre_unit(kind, ex, idx):
                    par = pstate["n"] % pstate["np"]
                    pstate["n"] += 1
                    ioff = 0 if ex != NEXP + 1 else 11
                    joff = 0 if ex != NEXP + 1 else 1408
                    if kind == "g":
                        if ex < NEXP:
                            srcv = wg_d[ex][idx * 128:(idx + 1) * 128, :]
                        else:
                            srcv = swg_d[idx * 128:(idx + 1) * 128,
                                         ioff * 128:ioff * 128 + 1408]
                        n = 1408
                        eng = "scalar"
                        dst = wgsc_d[ex][:, :, idx * 128:(idx + 1) * 128]
                    elif kind == "u":
                        if ex < NEXP:
                            srcv = wu_d[ex][idx * 128:(idx + 1) * 128, :]
                        else:
                            srcv = swu_d[idx * 128:(idx + 1) * 128,
                                         ioff * 128:ioff * 128 + 1408]
                        n = 1408
                        eng = "gpsimd"
                        dst = wusc_d[ex][:, :, idx * 128:(idx + 1) * 128]
                    else:
                        if ex < NEXP:
                            srcv = wd_d[ex][idx * 128:(idx + 1) * 128, :]
                        else:
                            srcv = swd_d[joff + idx * 128:
                                         joff + (idx + 1) * 128, :]
                        n = 2048
                        eng = "scalar"
                        dst = wdsc_d[ex][:, :, idx * 128:(idx + 1) * 128]
                    fv = pstate["pf"][par][:, 0:n]
                    bv = pstate["pb"][par][:, 0:n]
                    E("sync", lambda e, s=srcv, d=fv: e.dma_start(
                        out=d, in_=s), writes=[pstate["pfB"][par]],
                      dma=f"dPF{par}")
                    if eng == "scalar":
                        E("scalar", lambda e, s=fv, d=bv: e.activation(
                            out=d, in_=s, func=AF.Identity),
                          reads=[pstate["pfB"][par]], writes=[pstate["pbB"][par]])
                    else:
                        E("gpsimd", lambda e, s=fv, d=bv: e.tensor_copy(
                            out=d, in_=s), reads=[pstate["pfB"][par]], writes=[pstate["pbB"][par]])
                    if kind == "d":
                        sview = bv.rearrange("p (hh x) -> p hh x", hh=16)
                        dview = dst.rearrange("hh p x -> p hh x")
                        wb = wdscB[ex]
                    else:
                        sview = bv.rearrange("p (i x) -> p i x", i=11)
                        dview = dst.rearrange("i p x -> p i x")
                        wb = wgscB[ex] if kind == "g" else wuscB[ex]
                    E("scalar", lambda e, s=sview, d=dview: e.dma_start(
                        out=d, in_=s), reads=[pstate["pbB"][par]],
                      writes=[wb], dma=f"dPW{par}")

                for ex in range(NEXP + 2):
                    for k in range(HC):
                        prelude_units.append(("g", ex, k))
                        prelude_units.append(("u", ex, k))
                    for j in range(ICE):
                        prelude_units.append(("d", ex, j))

            for ti in range(NT):
                xb = ti % 2
                xt = x_tm[xb]
                E("sync", lambda e, d=xt, r0=tok0 + ti * 128:
                  e.dma_start(out=d, in_=xin[r0:r0 + 128, :]),
                  writes=[x_tmB[xb]], dma=f"dX{xb}")
                E("vector", lambda e, x=xt: e.reduce_sum(
                    out=sc["rs"][0], in_=x, axis=AX.X),
                  reads=[x_tmB[xb]], writes=[sc["rs"][1]])
                E("vector", lambda e, x=xt, s=xsq: e.scalar_tensor_tensor(
                    out=s, in0=x, scalar=1.0, in1=x, op0=ALU.mult,
                    op1=ALU.mult, accum_out=sc["rq"][0]),
                  reads=[x_tmB[xb]], writes=[xsqB, sc["rq"][1]])
                E("vector", lambda e: e.tensor_scalar_mul(
                    out=sc["mean"][0], in0=sc["rs"][0], scalar1=1.0 / 2048),
                  reads=[sc["rs"][1]], writes=[sc["mean"][1]])
                E("vector", lambda e: e.tensor_scalar_mul(
                    out=sc["msq"][0], in0=sc["rq"][0], scalar1=1.0 / 2048),
                  reads=[sc["rq"][1]], writes=[sc["msq"][1]])
                E("vector", lambda e: e.tensor_mul(
                    out=sc["m2"][0], in0=sc["mean"][0], in1=sc["mean"][0]),
                  reads=[sc["mean"][1]], writes=[sc["m2"][1]])
                E("vector", lambda e: e.tensor_sub(
                    out=sc["var"][0], in0=sc["msq"][0], in1=sc["m2"][0]),
                  reads=[sc["msq"][1], sc["m2"][1]], writes=[sc["var"][1]])
                E("scalar", lambda e: e.activation(
                    out=sc["std"][0], in_=sc["var"][0], func=AF.Sqrt,
                    bias=epsv, scale=1.0),
                  reads=[sc["var"][1], epsB], writes=[sc["std"][1]])
                E("vector", lambda e: e.reciprocal(
                    out=sc["rstd"][0], in_=sc["std"][0]),
                  reads=[sc["std"][1]], writes=[sc["rstd"][1]])
                E("vector", lambda e: e.tensor_scalar(
                    out=sc["mrneg"][0], in0=sc["mean"][0],
                    scalar1=sc["rstd"][0], scalar2=-1.0,
                    op0=ALU.mult, op1=ALU.mult),
                  reads=[sc["mean"][1], sc["rstd"][1]], writes=[sc["mrneg"][1]])
                E("scalar", lambda e, x=xt: e.activation(
                    out=xln_tm, in_=x, func=AF.Identity,
                    bias=sc["mrneg"][0], scale=sc["rstd"][0]),
                  reads=[x_tmB[xb], sc["mrneg"][1], sc["rstd"][1]],
                  writes=[xln_tmB])
                for k in range(HC):
                    bk = k % 2
                    pv = banks[bk][:, 0:128]
                    E("tensor", lambda e, x=xt, k=k, pv=pv: e.transpose(
                        out=pv, in_=x[:, k * 128:(k + 1) * 128], identity=eyef),
                      reads=[x_tmB[xb], eyefB], writes=[bankB[bk]])
                    dst = A[:, k * T_PASS + ti * 128: k * T_PASS + ti * 128 + 128]
                    E("vector", lambda e, pv=pv, dst=dst: e.tensor_copy(
                        out=dst, in_=pv), reads=[bankB[bk]], writes=[Ab[k]])
                for k in range(HC):
                    bk = 2 + k % 2
                    pv = banks[bk].bitcast(BF16)[:, 0:128]
                    E("tensor", lambda e, k=k, pv=pv: e.transpose(
                        out=pv, in_=xln_tm[:, k * 128:(k + 1) * 128],
                        identity=eyeb),
                      reads=[xln_tmB, eyebB], writes=[bankB[bk]])
                    dst = xln_fm[:, k * 128:(k + 1) * 128]
                    E("vector", lambda e, pv=pv, dst=dst: e.tensor_copy(
                        out=dst, in_=pv), reads=[bankB[bk]], writes=[xln_fmB])
                qkv3v = qkv_tm.rearrange("p (s c d) -> p s c d", s=3, c=32)
                for p in range(16):
                    bk = 4 + p % 2
                    pv = banks[bk][:, 0:384]

                    def qkvmm(e, p=p, pv=pv):
                        e.matmul(out=pv, lhsT=xln_fm[:, p * 128:(p + 1) * 128],
                                 rhs=wqkvbd[:, p * 384:(p + 1) * 384],
                                 start=True, stop=False)
                        return e.matmul(out=pv, lhsT=onesrowb,
                                        rhs=biasrow[0:1, p * 384:(p + 1) * 384],
                                        start=False, stop=True)
                    E("tensor", qkvmm,
                      reads=[xln_fmB, wqkvbdB, onesrowbB, biasrowB],
                      writes=[bankB[bk]])
                    src = pv.rearrange("p (s c d) -> p s c d", s=3, c=2)
                    dst = qkv3v[:, :, 2 * p:2 * p + 2, :]
                    E("vector", lambda e, src=src, dst=dst: e.tensor_copy(
                        out=dst, in_=src), reads=[bankB[bk]], writes=[qkv_tmB])
                def st_smult(dc):
                    pb = dc % 2
                    d0 = dc * DCH
                    qv = qkv_tm.rearrange("p (s c d) -> p s d c", s=3, c=32)[
                        :, 0, d0:d0 + DCH, :].unsqueeze(3).broadcast_to(
                        [128, DCH, 32, 32])
                    kv = qkv_tm.rearrange("p (s c d) -> p s d c", s=3, c=32)[
                        :, 1, d0:d0 + DCH, :].unsqueeze(2).broadcast_to(
                        [128, DCH, 32, 32])
                    s_out = S2[pb].rearrange("p (d c e) -> p d c e",
                                             d=DCH, c=32)
                    E("vector", lambda e, qv=qv, kv=kv, s_out=s_out:
                      e.tensor_tensor(out=s_out, in0=qv, in1=kv, op=ALU.mult),
                      reads=[qkv_tmB], writes=[S2B[pb]])

                def st_exp(dc):
                    pb = dc % 2
                    E("scalar", lambda e, s=S2[pb], p=P2[pb]: e.activation(
                        out=p, in_=s, func=AF.Exp, scale=0.125),
                      reads=[S2B[pb]], writes=[P2B[pb]])

                def st_esum(dc):
                    pb = dc % 2
                    d0 = dc * DCH
                    pview = P2[pb].rearrange("p (d c e) -> p d c e",
                                             d=DCH, c=32)
                    E("vector", lambda e, pview=pview, o=sums2[pb]:
                      e.reduce_sum(out=o.rearrange("p (d c) -> p d c", d=DCH),
                                   in_=pview, axis=AX.X),
                      reads=[P2B[pb]], writes=[sums2B[pb]])
                    E("vector", lambda e, i_=sums2[pb], o=rc2[pb]:
                      e.reciprocal(out=o, in_=i_),
                      reads=[sums2B[pb]], writes=[rc2B[pb]])
                    vsl = qkv_tm.rearrange("p (s c d) -> p s c d", s=3, c=32)[
                        :, 2, :, d0:d0 + DCH]
                    rcv = rc2[pb].rearrange("p (d c) -> p c d", d=DCH)
                    E("vector", lambda e, vsl=vsl, rcv=rcv,
                      o=v2[pb].rearrange("p (c d) -> p c d", c=32):
                      e.tensor_tensor(out=o, in0=vsl, in1=rcv, op=ALU.mult),
                      reads=[qkv_tmB, rc2B[pb]], writes=[v2B[pb]])

                def st_gmul(dc):
                    pb = dc % 2
                    vbc = v2[pb].rearrange("p (c d) -> p d c", c=32).unsqueeze(
                        3).broadcast_to([128, DCH, 32, 32])
                    E("gpsimd", lambda e, p=P2[pb], vbc=vbc,
                      o=tmp2[pb].rearrange("p (d c m) -> p d c m", d=DCH, c=32):
                      e.tensor_tensor(out=o, in0=p.rearrange(
                          "p (d c m) -> p d c m", d=DCH, c=32), in1=vbc,
                          op=ALU.mult),
                      reads=[P2B[pb], v2B[pb]], writes=[tmp2B[pb]])

                def st_csum(dc):
                    pb = dc % 2
                    d0 = dc * DCH
                    tview = tmp2[pb].rearrange("p (d c m) -> p d m c",
                                               d=DCH, c=32)
                    cview = ctx.rearrange("p (m d) -> p d m", m=32)[
                        :, d0:d0 + DCH, :]
                    E("vector", lambda e, tview=tview, cview=cview:
                      e.reduce_sum(out=cview, in_=tview, axis=AX.X),
                      reads=[tmp2B[pb]], writes=[ctxB])

                for i in range(NDC + 2):
                    if i < NDC:
                        st_smult(i)
                        st_exp(i)
                    if 1 <= i and i - 1 < NDC:
                        st_esum(i - 1)
                        st_gmul(i - 1)
                    if 2 <= i:
                        st_csum(i - 2)
                if debug and h == 0 and ti == 0:
                    E("sync", lambda e: e.dma_start(out=dbgQ, in_=qkv_tm),
                      reads=[qkv_tmB], dma="dDBG", extra=chain("dDBG"))
                    E("sync", lambda e: e.dma_start(out=dbgX, in_=ctx),
                      reads=[ctxB], dma="dDBG", extra=chain("dDBG"))
                for j in range(16):
                    bk = 2 + j % 2
                    pv = banks[bk][:, 128:256]
                    E("tensor", lambda e, j=j, pv=pv: e.transpose(
                        out=pv, in_=ctx[:, j * 128:(j + 1) * 128],
                        identity=eyef),
                      reads=[ctxB, eyefB], writes=[bankB[bk]])
                    cf = ctx_fm[j % 2]
                    E("vector", lambda e, pv=pv, cf=cf: e.tensor_copy(
                        out=cf, in_=pv), reads=[bankB[bk]],
                      writes=[ctx_fmB[j % 2]])
                    bo = 6 + j % 2
                    po = banks[bo][:, 0:128]
                    E("tensor", lambda e, cf=cf, po=po: e.matmul(
                        out=po, lhsT=woutbd, rhs=cf, start=True, stop=True),
                      reads=[ctx_fmB[j % 2], woutbdB], writes=[bankB[bo]])
                    tb = t3[j % 2]
                    E("scalar", lambda e, po=po, tb=tb: e.activation(
                        out=tb, in_=po, func=AF.Identity, bias=bout2,
                        scale=1.0),
                      reads=[bankB[bo], bout2B], writes=[t3B[j % 2]])
                    asl = A[:, j * T_PASS + ti * 128: j * T_PASS + ti * 128 + 128]
                    E("vector", lambda e, asl=asl, tb=tb: e.tensor_add(
                        out=asl, in0=asl, in1=tb),
                      reads=[t3B[j % 2], Ab[j]], writes=[Ab[j]])
                if prelude_units:
                    ntake = min(17, len(prelude_units))
                    for u in prelude_units[:ntake]:
                        pre_unit(*u)
                    del prelude_units[:ntake]

            if debug:
                E("sync", lambda e, h=h: e.dma_start(out=dbgA[h], in_=A),
                  reads=list(Ab), dma="dDBG", extra=chain("dDBG"))
            arena.begin("ffn")
            pf2, pf2B = arena.alloc("pf2", 128, 2048, F32)
            pb2, pb2B = arena.alloc("pb2", 128, 2048, BF16)
            if h == 0:
                pstate["pf"] = [pf2]; pstate["pfB"] = [pf2B]
                pstate["pb"] = [pb2]; pstate["pbB"] = [pb2B]
                pstate["np"] = 1
            ln2x, _ln2 = arena.alloc("ln2x", 128, HC * T_PASS, BF16)
            ln2xB = [Buf(f"ln2x{k}") for k in range(HC)]
            abuf, _ab = arena.alloc("abuf", 128, ICE * T_PASS, BF16)
            abufB = [Buf(f"abuf{j}") for j in range(ICE)]
            wgb, wgbB = zip(*[arena.alloc(f"wgb{i}", 128, 2048, BF16)
                              for i in range(2)])
            wub, wubB = zip(*[arena.alloc(f"wub{i}", 128, 2048, BF16)
                              for i in range(2)])
            wdb, wdbB = zip(*[arena.alloc(f"wdb{i}", 128, 1408, BF16)
                              for i in range(2)])
            s_t, s_tB = zip(*[arena.alloc(f"s_t{i}", 128, 512, F32)
                              for i in range(2)])
            t5, t5B = zip(*[arena.alloc(f"t5_{i}", 128, 512, BF16)
                            for i in range(2)])
            C_b, C_bB = arena.alloc("C_b", 128, 1024, BF16)
            R_b, R_bB = arena.alloc("R_b", 128, 512, F32)
            MR_b, MR_bB = arena.alloc("MR_b", 128, 512, F32)
            rows = {}
            for nm in ("mrow", "msqrow", "ra", "rb"):
                rows[nm] = arena.alloc("r_" + nm, 1, 512, F32)
            comb_all, comb_allB = arena.alloc("comb_all", 128, 128, F32)
            comb_fm, comb_fmB = arena.alloc("comb_fm", 16, 1024, BF16)
            tk = {}
            for nm in ("sorig", "swork", "msk", "acc4"):
                tk[nm] = arena.alloc("tk_" + nm, 128, 16, F32)
            for nm in ("mx", "nmx", "z4", "rz"):
                tk[nm] = arena.alloc("tk_" + nm, 128, 1, F32)

            for blk in range(NB):
                c0 = blk * 512

                def statmms(e, c0=c0):
                    last = None
                    for k in range(HC):
                        last = e.matmul(
                            out=banks[0][0:1, :], lhsT=onescol,
                            rhs=A[:, k * T_PASS + c0: k * T_PASS + c0 + 512],
                            start=(k == 0), stop=(k == HC - 1))
                    return last
                E("tensor", statmms, reads=list(Ab) + [onescolB],
                  writes=[bankB[0]])
                for k in range(HC):
                    sq = s_t[k % 2]
                    E("scalar", lambda e, k=k, sq=sq, c0=c0: e.activation(
                        out=sq, in_=A[:, k * T_PASS + c0: k * T_PASS + c0 + 512],
                        func=AF.Square),
                      reads=[Ab[k]], writes=[s_tB[k % 2]])
                    E("tensor", lambda e, sq=sq, k=k: e.matmul(
                        out=banks[1][0:1, :], lhsT=onescol, rhs=sq,
                        start=(k == 0), stop=(k == HC - 1)),
                      reads=[s_tB[k % 2], onescolB], writes=[bankB[1]])
                E("scalar", lambda e: e.activation(
                    out=rows["mrow"][0], in_=banks[0][0:1, :], func=AF.Identity,
                    bias=0.0, scale=1.0 / 2048),
                  reads=[bankB[0]], writes=[rows["mrow"][1]])
                E("scalar", lambda e: e.activation(
                    out=rows["msqrow"][0], in_=banks[1][0:1, :],
                    func=AF.Identity, bias=0.0, scale=1.0 / 2048),
                  reads=[bankB[1]], writes=[rows["msqrow"][1]])
                E("vector", lambda e: e.tensor_mul(
                    out=rows["ra"][0], in0=rows["mrow"][0], in1=rows["mrow"][0]),
                  reads=[rows["mrow"][1]], writes=[rows["ra"][1]])
                E("vector", lambda e: e.tensor_sub(
                    out=rows["rb"][0], in0=rows["msqrow"][0], in1=rows["ra"][0]),
                  reads=[rows["msqrow"][1], rows["ra"][1]],
                  writes=[rows["rb"][1]])
                E("scalar", lambda e: e.activation(
                    out=rows["ra"][0], in_=rows["rb"][0], func=AF.Sqrt,
                    bias=epsv[0:1, :], scale=1.0),
                  reads=[rows["rb"][1], epsB], writes=[rows["ra"][1]])
                E("vector", lambda e: e.reciprocal(
                    out=rows["rb"][0], in_=rows["ra"][0]),
                  reads=[rows["ra"][1]], writes=[rows["rb"][1]])
                E("vector", lambda e: e.tensor_mul(
                    out=rows["ra"][0], in0=rows["mrow"][0], in1=rows["rb"][0]),
                  reads=[rows["mrow"][1], rows["rb"][1]],
                  writes=[rows["ra"][1]])
                E("tensor", lambda e: e.matmul(
                    out=banks[2][:, :], lhsT=onesrowf, rhs=rows["rb"][0],
                    start=True, stop=True),
                  reads=[rows["rb"][1], onesrowfB], writes=[bankB[2]])
                E("tensor", lambda e: e.matmul(
                    out=banks[3][:, :], lhsT=onesrowf, rhs=rows["ra"][0],
                    start=True, stop=True),
                  reads=[rows["ra"][1], onesrowfB], writes=[bankB[3]])
                E("vector", lambda e: e.tensor_copy(out=R_b, in_=banks[2][:, :]),
                  reads=[bankB[2]], writes=[R_bB])
                E("vector", lambda e: e.tensor_copy(out=MR_b, in_=banks[3][:, :]),
                  reads=[bankB[3]], writes=[MR_bB])
                for k in range(HC):
                    sq = s_t[k % 2]
                    asl = A[:, k * T_PASS + c0: k * T_PASS + c0 + 512]
                    E("vector", lambda e, asl=asl, sq=sq: e.tensor_mul(
                        out=sq, in0=asl, in1=R_b),
                      reads=[Ab[k], R_bB], writes=[s_tB[k % 2]])
                    lsl = ln2x[:, k * T_PASS + c0: k * T_PASS + c0 + 512]
                    E("vector", lambda e, lsl=lsl, sq=sq: e.tensor_sub(
                        out=lsl, in0=sq, in1=MR_b),
                      reads=[s_tB[k % 2], MR_bB], writes=[ln2xB[k]])

            for ts in range(NT):
                def gatemms(e, ts=ts):
                    last = None
                    for k in range(HC):
                        last = e.matmul(
                            out=banks[4][:, 0:16],
                            lhsT=ln2x[:, k * T_PASS + ts * 128:
                                      k * T_PASS + ts * 128 + 128],
                            rhs=gtw[:, k * 16:(k + 1) * 16],
                            start=(k == 0), stop=(k == HC - 1))
                    return last
                E("tensor", gatemms, reads=list(ln2xB) + [gtwB],
                  writes=[bankB[4]])
                lg = banks[4][:, 0:16]
                E("vector", lambda e, lg=lg: e.reduce_max(
                    out=tk["mx"][0], in_=lg, axis=AX.X),
                  reads=[bankB[4]], writes=[tk["mx"][1]])
                E("vector", lambda e: e.tensor_scalar_mul(
                    out=tk["nmx"][0], in0=tk["mx"][0], scalar1=-1.0),
                  reads=[tk["mx"][1]], writes=[tk["nmx"][1]])
                E("scalar", lambda e, lg=lg: e.activation(
                    out=tk["sorig"][0], in_=lg, func=AF.Exp,
                    bias=tk["nmx"][0], scale=1.0),
                  reads=[bankB[4], tk["nmx"][1]], writes=[tk["sorig"][1]])
                E("vector", lambda e: e.tensor_copy(
                    out=tk["swork"][0], in_=tk["sorig"][0]),
                  reads=[tk["sorig"][1]], writes=[tk["swork"][1]])
                E("vector", lambda e: e.memset(tk["acc4"][0], 0.0),
                  writes=[tk["acc4"][1]])
                for it in range(4):
                    E("vector", lambda e: e.reduce_max(
                        out=tk["mx"][0], in_=tk["swork"][0], axis=AX.X),
                      reads=[tk["swork"][1]], writes=[tk["mx"][1]])
                    E("vector", lambda e: e.tensor_scalar(
                        out=tk["msk"][0], in0=tk["swork"][0],
                        scalar1=tk["mx"][0], scalar2=None, op0=ALU.is_equal),
                      reads=[tk["swork"][1], tk["mx"][1]], writes=[tk["msk"][1]])
                    E("vector", lambda e: e.scalar_tensor_tensor(
                        out=tk["swork"][0], in0=tk["msk"][0], scalar=-1e30,
                        in1=tk["swork"][0], op0=ALU.mult, op1=ALU.add),
                      reads=[tk["msk"][1], tk["swork"][1]],
                      writes=[tk["swork"][1]])
                    E("vector", lambda e: e.tensor_add(
                        out=tk["acc4"][0], in0=tk["acc4"][0], in1=tk["msk"][0]),
                      reads=[tk["msk"][1], tk["acc4"][1]],
                      writes=[tk["acc4"][1]])
                E("vector", lambda e: e.tensor_mul(
                    out=tk["msk"][0], in0=tk["acc4"][0], in1=tk["sorig"][0]),
                  reads=[tk["acc4"][1], tk["sorig"][1]], writes=[tk["msk"][1]])
                E("vector", lambda e: e.reduce_sum(
                    out=tk["z4"][0], in_=tk["msk"][0], axis=AX.X),
                  reads=[tk["msk"][1]], writes=[tk["z4"][1]])
                E("vector", lambda e: e.reciprocal(
                    out=tk["rz"][0], in_=tk["z4"][0]),
                  reads=[tk["z4"][1]], writes=[tk["rz"][1]])
                E("vector", lambda e, ts=ts: e.tensor_scalar_mul(
                    out=comb_all[:, ts * 16:(ts + 1) * 16], in0=tk["msk"][0],
                    scalar1=tk["rz"][0]),
                  reads=[tk["msk"][1], tk["rz"][1]], writes=[comb_allB])
            for ts in range(NT):
                pv = banks[5][0:16, 0:128]
                E("tensor", lambda e, ts=ts, pv=pv: e.transpose(
                    out=pv, in_=comb_all[:, ts * 16:(ts + 1) * 16],
                    identity=eyef),
                  reads=[comb_allB, eyefB], writes=[bankB[5]])
                E("vector", lambda e, ts=ts, pv=pv: e.tensor_copy(
                    out=comb_fm[:, ts * 128:(ts + 1) * 128], in_=pv),
                  reads=[bankB[5]], writes=[comb_fmB])

            if debug:
                E("sync", lambda e, h=h, l=ln2x: e.dma_start(out=dbgL[h], in_=l),
                  reads=list(ln2xB), dma="dDBG", extra=chain("dDBG"))
                E("sync", lambda e, h=h, c=comb_all: e.dma_start(out=dbgC[h], in_=c),
                  reads=[comb_allB], dma="dDBG", extra=chain("dDBG"))
            for ex in range(NEXP + 2):
                shared = ex >= NEXP
                ioff = 0 if ex != NEXP + 1 else 11
                joff = 0 if ex != NEXP + 1 else 1408
                if not shared:
                    for blk in range(NB):
                        E("tensor", lambda e, ex=ex, blk=blk: e.matmul(
                            out=banks[6][:, :],
                            lhsT=sel[:, ex * 128:(ex + 1) * 128],
                            rhs=comb_fm[:, blk * 512:(blk + 1) * 512],
                            start=True, stop=True),
                          reads=[selB, comb_fmB], writes=[bankB[6]])
                        E("vector", lambda e, blk=blk: e.tensor_copy(
                            out=C_b[:, blk * 512:(blk + 1) * 512],
                            in_=banks[6][:, :]),
                          reads=[bankB[6]], writes=[C_bB])
                for isl in range(ICE):
                    par = isl % 2
                    E("sync", lambda e, s=wgsc_d[ex][isl], d=wgb[par]:
                      e.dma_start(out=d, in_=s),
                      reads=[wgscB[ex]], writes=[wgbB[par]], dma=f"dG{par}")
                    E("scalar", lambda e, s=wusc_d[ex][isl], d=wub[par]:
                      e.dma_start(out=d, in_=s),
                      reads=[wuscB[ex]], writes=[wubB[par]], dma=f"dU{par}")
                    if prelude_units:
                        for u in prelude_units[:2]:
                            pre_unit(*u)
                        del prelude_units[:2]
                    gb0, gb1 = 4 * par, 4 * par + 1
                    ub0, ub1 = 4 * par + 2, 4 * par + 3

                    def gumms(e, par=par):
                        last = None
                        for k in range(HC):
                            st_, sp = (k == 0), (k == HC - 1)
                            for blk in range(NB):
                                gbk = banks[4 * par + blk]
                                ubk = banks[4 * par + 2 + blk]
                                e.matmul(
                                    out=gbk[:, :],
                                    lhsT=wgb[par][:, k * 128:(k + 1) * 128],
                                    rhs=ln2x[:, k * T_PASS + blk * 512:
                                             k * T_PASS + blk * 512 + 512],
                                    start=st_, stop=sp)
                                last = e.matmul(
                                    out=ubk[:, :],
                                    lhsT=wub[par][:, k * 128:(k + 1) * 128],
                                    rhs=ln2x[:, k * T_PASS + blk * 512:
                                             k * T_PASS + blk * 512 + 512],
                                    start=st_, stop=sp)
                        return last
                    E("tensor", gumms,
                      reads=[wgbB[par], wubB[par]] + list(ln2xB),
                      writes=[bankB[gb0], bankB[gb1], bankB[ub0], bankB[ub1]])
                    for blk in range(NB):
                        gbk = banks[4 * par + blk]
                        ubk = banks[4 * par + 2 + blk]
                        gB, uB = bankB[4 * par + blk], bankB[4 * par + 2 + blk]
                        stv = s_t[blk]
                        E("scalar", lambda e, gbk=gbk, stv=stv: e.activation(
                            out=stv, in_=gbk[:, :], func=AF.Silu),
                          reads=[gB], writes=[s_tB[blk]])
                        aslice = abuf[:, isl * T_PASS + blk * 512:
                                      isl * T_PASS + blk * 512 + 512]
                        if shared:
                            E("vector", lambda e, stv=stv, ubk=ubk,
                              aslice=aslice: e.tensor_mul(
                                  out=aslice, in0=stv, in1=ubk[:, :]),
                              reads=[s_tB[blk], uB], writes=[abufB[isl]])
                        else:
                            tv = t5[blk]
                            E("vector", lambda e, stv=stv, ubk=ubk, tv=tv:
                              e.tensor_mul(out=tv, in0=stv, in1=ubk[:, :]),
                              reads=[s_tB[blk], uB], writes=[t5B[blk]])
                            E("gpsimd", lambda e, tv=tv, aslice=aslice,
                              blk=blk: e.tensor_tensor(
                                  out=aslice, in0=tv,
                                  in1=C_b[:, blk * 512:(blk + 1) * 512],
                                  op=ALU.mult),
                              reads=[t5B[blk], C_bB], writes=[abufB[isl]])
                for hc in range(HC):
                    par = hc % 2
                    E("sync", lambda e, s=wdsc_d[ex][hc], d=wdb[par]:
                      e.dma_start(out=d, in_=s),
                      reads=[wdscB[ex]], writes=[wdbB[par]], dma=f"dD{par}")
                    if prelude_units:
                        for u in prelude_units[:2]:
                            pre_unit(*u)
                        del prelude_units[:2]
                    yb0, yb1 = 4 + 2 * par, 5 + 2 * par

                    def dmms(e, par=par):
                        last = None
                        for j in range(ICE):
                            st_, sp = (j == 0), (j == ICE - 1)
                            for blk in range(NB):
                                ybk = banks[4 + 2 * par + blk]
                                last = e.matmul(
                                    out=ybk[:, :],
                                    lhsT=wdb[par][:, j * 128:(j + 1) * 128],
                                    rhs=abuf[:, j * T_PASS + blk * 512:
                                             j * T_PASS + blk * 512 + 512],
                                    start=st_, stop=sp)
                        return last
                    E("tensor", dmms, reads=[wdbB[par]] + list(abufB),
                      writes=[bankB[yb0], bankB[yb1]])
                    for blk in range(NB):
                        ybk = banks[4 + 2 * par + blk]
                        yB = bankB[4 + 2 * par + blk]
                        asl = A[:, hc * T_PASS + blk * 512:
                                hc * T_PASS + blk * 512 + 512]
                        E("vector", lambda e, asl=asl, ybk=ybk: e.tensor_add(
                            out=asl, in0=asl, in1=ybk[:, :]),
                          reads=[yB, Ab[hc]], writes=[Ab[hc]])

            arena.begin("epi")
            out_tm, out_tmB = zip(*[arena.alloc(f"out_tm{i}", 128, 2048, F32)
                                    for i in range(2)])
            for ti in range(NT):
                ob = ti % 2
                for k in range(HC):
                    bk = k % 2
                    pv = banks[bk][:, 0:128]
                    E("tensor", lambda e, k=k, ti=ti, pv=pv: e.transpose(
                        out=pv,
                        in_=A[:, k * T_PASS + ti * 128: k * T_PASS + ti * 128 + 128],
                        identity=eyef),
                      reads=[Ab[k], eyefB], writes=[bankB[bk]])
                    E("vector", lambda e, k=k, pv=pv, d=out_tm[ob]:
                      e.tensor_copy(out=d[:, k * 128:(k + 1) * 128], in_=pv),
                      reads=[bankB[bk]], writes=[out_tmB[ob]])
                E("sync", lambda e, d=out_tm[ob], r0=tok0 + ti * 128:
                  e.dma_start(out=out_d[r0:r0 + 128, :], in_=d),
                  reads=[out_tmB[ob]], dma=f"dO{ob}")

        if debug:
            dbgW1 = nc.dram_tensor("dbgW1", [11, 128, 2048], BF16,
                                   kind="ExternalOutput").ap()
            dbgW5 = nc.dram_tensor("dbgW5", [11, 128, 2048], BF16,
                                   kind="ExternalOutput").ap()
            dbgD5 = nc.dram_tensor("dbgD5", [16, 128, 1408], BF16,
                                   kind="ExternalOutput").ap()
            plan.wait_only("sync", [(s_, plan.cnt.get(s_, 0))
                                    for s_ in plan.cnt])
            E("sync", lambda e: e.dma_start(out=dbgW1, in_=wgsc_d[1]),
              dma="dDBG", extra=chain("dDBG"))
            E("sync", lambda e: e.dma_start(out=dbgW5, in_=wgsc_d[5]),
              dma="dDBG", extra=chain("dDBG"))
            E("sync", lambda e: e.dma_start(out=dbgD5, in_=wdsc_d[5]),
              dma="dDBG", extra=chain("dDBG"))
        plan.wait_only("sync", [(s, plan.cnt.get(s, 0)) for s in sem_names
                                if s.startswith("d")])

        def run_ops(handle, eng):
            for fn, waits, sem, amt in plan.ops[eng]:
                for s, v in waits:
                    handle.wait_ge(sems[s], v)
                if fn is not None:
                    ins = fn(handle)
                    if sem is not None:
                        ins.then_inc(sems[sem], amt)

        @block.sync
        def _(e):
            run_ops(e, "sync")

        @block.vector
        def _(e):
            run_ops(e, "vector")

        @block.scalar
        def _(e):
            run_ops(e, "scalar")

        @block.tensor
        def _(e):
            run_ops(e, "tensor")

        @block.gpsimd
        def _(e):
            run_ops(e, "gpsimd")

    return nc


def host_prepare(inputs):
    Wqkv = np.asarray(inputs["L0_attn_Wqkv"], dtype=np.float32)
    bqkv = np.asarray(inputs["L0_attn_bqkv"], dtype=np.float32)
    Wout = np.asarray(inputs["L0_attn_Wout"], dtype=np.float32)
    bout = np.asarray(inputs["L0_attn_bout"], dtype=np.float32)
    gate_w = np.asarray(inputs["L0_ffn_gate_w"], dtype=np.float32)

    wbd = np.zeros((16, 2, 64, 3, 2, 64), dtype=np.float32)
    wq5 = Wqkv.reshape(16, 2, 64, 3, 64)
    for c2 in range(2):
        wbd[:, c2, :, :, c2, :] = wq5[:, c2]
    wqkv_bd = wbd.reshape(16, 128, 384)

    bias_pairs = bqkv.reshape(16, 2, 3, 64).transpose(0, 2, 1, 3).reshape(1, 6144)

    wout_bd = np.zeros((128, 128), dtype=np.float32)
    wout_bd[0:64, 0:64] = Wout
    wout_bd[64:128, 64:128] = Wout

    bout2 = np.concatenate([bout, bout]).reshape(128, 1)
    gate_wT = np.ascontiguousarray(gate_w.T)
    eye128 = np.eye(128, dtype=np.float32)

    return {
        "w_gate": np.ascontiguousarray(inputs["L0_ffn_w_gate"], dtype=np.float32),
        "w_up": np.ascontiguousarray(inputs["L0_ffn_w_up"], dtype=np.float32),
        "w_down": np.ascontiguousarray(inputs["L0_ffn_w_down"], dtype=np.float32),
        "sw_gate": np.ascontiguousarray(inputs["L0_ffn_sw_gate"], dtype=np.float32),
        "sw_up": np.ascontiguousarray(inputs["L0_ffn_sw_up"], dtype=np.float32),
        "sw_down": np.ascontiguousarray(inputs["L0_ffn_sw_down"], dtype=np.float32),
        "wqkv_bd": wqkv_bd,
        "bias_pairs": bias_pairs,
        "wout_bd": wout_bd,
        "bout2": bout2,
        "gate_wT": gate_wT,
        "eye128": eye128,
    }


def kernel(**inputs):
    x = np.asarray(inputs["L0_attn_x"], dtype=np.float32).reshape(16384, 2048)
    shared = host_prepare(inputs)
    nc = build_program()
    in_maps = []
    for c in range(N_CORES):
        m = dict(shared)
        m["xin"] = np.ascontiguousarray(x[c * T_CORE:(c + 1) * T_CORE, :])
        in_maps.append(m)
    res = run_bass_kernel_spmd(nc, in_maps, core_ids=list(range(N_CORES)))
    out = np.concatenate([r["out"] for r in res.results], axis=0)
    return out.reshape(1, 16384, 2048)


if __name__ == "__main__":
    nc = build_program()
    print("build ok")
